# version 05
# baseline (speedup 1.0000x reference)
import numpy as np
import ml_dtypes
import concourse.bass as bass
import concourse.mybir as mybir
from concourse.bass_utils import run_bass_kernel_spmd

F32 = mybir.dt.float32
BF16 = mybir.dt.bfloat16
U32 = mybir.dt.uint32
AF = mybir.ActivationFunctionType
ALU = mybir.AluOpType
AX = mybir.AxisListType

DM = 1024
DI = 2048
DST = 16
NE = 32
HE = 128
HSH = 512
NCORE = 8
BIGNEG = 1.0e30
LN_EPS = 1.0e-5
NKT_DM = DM // 128
NKT_DI = DI // 128


class CFG:
    def __init__(self, T=16384, TF=512, TS=512):
        self.T = T
        self.TPC = T // 4
        self.TF = TF
        self.TS = TS
        self.NCHF = self.TPC // TF
        self.NCHS = T // TS
        self.SBK = 4
        self.NSB = DST // self.SBK
        self.DG = DI // 4
        self.NDT = self.DG // 128


class Walk:
    ENGS = ("sy", "gp", "ve", "se", "te")

    def __init__(self, nc, E, sems):
        self.nc = nc
        self.E = E
        self.sems = sems
        self.c = {k: 0 for k in sems}
        self.eng = {"sy": nc.sync, "gp": nc.gpsimd, "ve": nc.vector,
                    "se": nc.scalar, "te": nc.tensor}[E]

    def snap(self, name):
        return self.c[name]

    def _bump(self, name, instr, k):
        self.c[name] += k
        if instr is not None:
            instr.then_inc(self.sems[name], k)
        return instr

    def wait(self, on_eng, name, val=None):
        v = self.c[name] if val is None else val
        if v <= 0:
            return
        if self.E == on_eng:
            self.eng.wait_ge(self.sems[name], v)

    def ve_(self, fn):
        return self._bump("VE", fn(self.nc.vector) if self.E == "ve" else None, 1)

    def se_(self, fn):
        return self._bump("SE", fn(self.nc.scalar) if self.E == "se" else None, 1)

    def te_(self, fn):
        return self._bump("TE", fn(self.nc.tensor) if self.E == "te" else None, 1)

    def gp_(self, fn):
        return self._bump("GP", fn(self.nc.gpsimd) if self.E == "gp" else None, 1)

    def dma_sy(self, out, in_, stream="LD"):
        i = self.nc.sync.dma_start(out=out, in_=in_) if self.E == "sy" else None
        return self._bump(stream, i, 16)

    def dma_se(self, out, in_):
        i = self.nc.scalar.dma_start(out=out, in_=in_) if self.E == "se" else None
        return self._bump("ST", i, 16)

    def dma_gp_dyn(self, fn):
        i = fn(self.nc.gpsimd) if self.E == "gp" else None
        return self._bump("GD", i, 16)

    def ag(self, ins, outs, groups):
        i = None
        if self.E == "gp":
            i = self.nc.gpsimd.collective_compute(
                "AllGather", ALU.bypass, replica_groups=groups, ins=ins, outs=outs)
        return self._bump("AG", i, 1)


def ceil32(x):
    return (x + 31) & ~31


class Arena:
    def __init__(self, t, cols, off=0):
        self.t = t
        self.cols = cols
        self.off = off

    def take(self, cols):
        o = ceil32(self.off)
        assert o + cols <= self.cols, \
            f"{self.t.name} overflow: {o}+{cols}>{self.cols}"
        self.off = o + cols
        return self.t[:, o:o + cols]


def build_program(cfg: CFG):
    nc = bass.Bass("TRN2", target_bir_lowering=False, debug=False)
    T, TF, TS, TPC = cfg.T, cfg.TF, cfg.TS, cfg.TPC
    NCHF, NCHS, SBK, NSB, DG, NDT = (cfg.NCHF, cfg.NCHS, cfg.SBK, cfg.NSB,
                                     cfg.DG, cfg.NDT)
    NTT = TF // 128

    def ein(name, shape, dt=F32):
        return nc.dram_tensor(name, shape, dt, kind="ExternalInput").ap()

    x_my = ein("x_my", [TPC, DM])
    W_in = ein("W_in", [DM, 2 * DI])
    b_in_u = ein("b_in_u", [128, DI // 128])
    b_in_z = ein("b_in_z", [128, DI // 128])
    W_x = ein("W_x", [DI, 2 * DST])
    b_x = ein("b_x", [2 * DST, 1])
    W_dt_sl = ein("W_dt_sl", [DI, DG])
    b_dt_sl = ein("b_dt_sl", [128, NDT])
    W_out = ein("W_out", [DI, DM])
    b_out_c = ein("b_out_c", [128, DM // 128])
    A_log_sl = ein("A_log_sl", [DG, DST])
    D_sl = ein("D_sl", [128, NDT])
    r_wT = ein("r_wT", [DM, NE])
    r_b = ein("r_b", [NE, 1])
    gate_wT = ein("gate_wT", [NE * DM, HE])
    up_wT = ein("up_wT", [NE * DM, HE])
    down_eT = ein("down_eT", [NE * HE, DM])
    sg_wT = ein("sg_wT", [DM, HSH])
    su_wT = ein("su_wT", [DM, HSH])
    sd_wT = ein("sd_wT", [HSH, DM])
    ident = ein("ident", [128, 128])
    ones_row = ein("ones_row", [1, 128])
    cinfo = ein("cinfo", [1, 4], U32)

    out_my = nc.dram_tensor("out_my", [TPC, DM], F32, kind="ExternalOutput").ap()

    def idram(name, shape, dt):
        return nc.dram_tensor(name, shape, dt, kind="Internal").ap()

    x1_d = idram("x1_d", [DM, TPC], F32)
    z_d = idram("z_d", [DI, TPC], BF16)
    u_in2 = idram("u_in2", [4, DI, TPC // 4], BF16)
    u_ag2 = idram("u_ag2", [4, 4 * DI, TPC // 4], BF16)
    y_in2 = idram("y_in2", [4, DG, T // 4], BF16)
    y_ag2 = idram("y_ag2", [4, DI, T // 4], BF16)
    bcT_d = idram("bcT_d", [2 * DST, T], BF16)
    wT_d = idram("wT_d", [NE, TPC], BF16)
    y_gs_d = idram("y_gs_d", [DI, TPC], BF16)
    win_bf = idram("win_bf", [DM, 2 * DI], BF16)
    gate_bf = idram("gate_bf", [NE * DM, HE], BF16)
    up_bf = idram("up_bf", [NE * DM, HE], BF16)
    down_bf = idram("down_bf", [NE * HE, DM], BF16)
    sg_bf = idram("sg_bf", [DM, HSH], BF16)
    su_bf = idram("su_bf", [DM, HSH], BF16)
    sd_bf = idram("sd_bf", [HSH, DM], BF16)
    wout_bf = idram("wout_bf", [DI, DM], BF16)
    rw_bf = idram("rw_bf", [DM, NE], BF16)

    identt = nc.alloc_sbuf_tensor("identt", [128, 128], F32)
    onesr = nc.alloc_sbuf_tensor("onesr", [1, 128], F32)
    onesc_bf = nc.alloc_sbuf_tensor("onesc_bf", [128, 1], BF16)
    cinfot = nc.alloc_sbuf_tensor("cinfot", [1, 4], U32)
    A_cols = nc.alloc_sbuf_tensor("A_cols", [128, NDT * DST], F32)
    D_col = nc.alloc_sbuf_tensor("D_col", [128, NDT], F32)
    bdt_col = nc.alloc_sbuf_tensor("bdt_col", [128, NDT], F32)
    bxc = nc.alloc_sbuf_tensor("bxc", [2 * DST, 1], F32)
    rbc = nc.alloc_sbuf_tensor("rbc", [NE, 1], F32)
    binu_c = nc.alloc_sbuf_tensor("binu_c", [128, DI // 128], F32)
    binz_c = nc.alloc_sbuf_tensor("binz_c", [128, DI // 128], F32)
    bout_c = nc.alloc_sbuf_tensor("bout_c", [128, DM // 128], F32)
    hcar = nc.alloc_sbuf_tensor("hcar", [128, NDT * DST], F32)
    wx_bf = nc.alloc_sbuf_tensor("wx_bf", [128, NKT_DI * 2 * DST], BF16)
    cst = nc.alloc_sbuf_tensor("cst", [128, 2], F32)

    F32_COLS = 20000
    BF_COLS = 64500
    af_t = nc.alloc_sbuf_tensor("af", [128, F32_COLS], F32)
    ab_t = nc.alloc_sbuf_tensor("ab", [128, BF_COLS], BF16)

    pA = nc.alloc_psum_tensor("pA", [128, 512], F32)
    pB = nc.alloc_psum_tensor("pB", [128, 512], F32)
    pC = nc.alloc_psum_tensor("pC", [128, 512], F32)
    pD = nc.alloc_psum_tensor("pD", [128, 512], F32)
    pE = nc.alloc_psum_tensor("pE", [128, 512], F32)
    pF = nc.alloc_psum_tensor("pF", [128, 512], F32)
    pG = nc.alloc_psum_tensor("pG", [128, 512], F32)
    pH = nc.alloc_psum_tensor("pH", [128, 512], F32)
    pALL = [pA, pB, pC, pD, pE, pF, pG, pH]

    sems = {}
    for s in ["LD", "LDR", "LDX", "U0", "U1", "B0", "B1", "W0", "W1",
              "I0", "I1", "S0", "S1", "O0", "O1", "DN", "RW",
              "ST", "GD", "VE", "SE", "TE", "GP", "AG"]:
        sems[s] = nc.semaphore(s).__enter__()

    af = Arena(af_t, F32_COLS)
    xtm = af.take(NTT * DM)
    x_res = af.take(NKT_DM * TF)
    x1c = af.take(NKT_DM * TF)
    mrow = af.take(TF)
    vrow = af.take(TF)
    rstdr = af.take(TF)
    m2row = af.take(TF)
    mrep = af.take(TF)
    rrep = af.take(TF)
    lgsb = af.take(TF)
    lgT = af.take(NTT * NE)
    wrk = af.take(NE)
    mks = af.take(4)
    nm1 = af.take(32)
    e4 = af.take(4)
    s4 = af.take(32)
    r4 = af.take(32)
    wv4 = af.take(4)
    w32 = af.take(NE)
    masks = af.take(4 * NE)
    wTsb = af.take(TF)
    h1t = af.take(2 * TF)
    tmp8 = xtm[:, 0:NKT_DM * TF]

    afp = Arena(af_t, F32_COLS)
    cast_f = afp.take(8192)

    afs = Arena(af_t, F32_COLS)
    ug_f = [afs.take(NDT * TS) for _ in range(2)]
    dt_f = [afs.take(NDT * TS) for _ in range(2)]
    yacc = afs.take(NDT * TS)
    yred = afs.take(TS)
    spA = [[afs.take(TS) for _ in range(2)] for _ in range(2)]

    ab = Arena(ab_t, BF_COLS)
    xbf = ab.take(NKT_DM * TF)
    xnbf = ab.take(NKT_DM * TF)
    hcache = ab.take(NE * TF)
    hsc = ab.take((HSH // 128) * TF)
    xsqbf = hcache[:, 0:NKT_DM * TF]
    wg4 = [ab.take(4 * NKT_DM * 128) for _ in range(2)]
    wu4 = [ab.take(4 * NKT_DM * 128) for _ in range(2)]
    wdn = [ab.take(NE * 128)]
    wshg = [ab.take(NKT_DM * 128) for _ in range(2)]
    wshu = [ab.take(NKT_DM * 128) for _ in range(2)]
    wshd = [ab.take((HSH // 128) * 128) for _ in range(2)]
    wint = [ab.take(NKT_DM * 128) for _ in range(2)]
    rwt = ab.take(NKT_DM * NE)
    sgbf = ab.take(2 * TF)
    wrep = ab.take(4 * TF)
    wTsb_bf = ab.take(TF)
    ubf_sb = ab.take(TF)
    zbf_sb = ab.take(TF)
    AB_END = ab.off

    abp = Arena(ab_t, BF_COLS)
    cast_b = abp.take(8192)

    abs_ = Arena(ab_t, BF_COLS)
    u_full = [abs_.take(NKT_DI * TS) for _ in range(2)]
    ybf_s = abs_.take(NDT * TS)
    wdt_bf = abs_.take(NKT_DI * NDT * 128)
    dtu_b = [abs_.take(NDT * TS) for _ in range(2)]
    bc_sb = [abs_.take(TS) for _ in range(2)]
    slab0 = abs_.off
    dA_bufs = [abs_.take(SBK * TS) for _ in range(2)]
    w_sl = abs_.take(SBK * TS)
    h_sl = abs_.take(SBK * TS)
    brep = [abs_.take(SBK * TS) for _ in range(2)]
    crep = [abs_.take(SBK * TS) for _ in range(2)]
    gbulk = ab_t[:, slab0:slab0 + 4 * TPC]

    abc = Arena(ab_t, BF_COLS, off=AB_END)
    ymk = abc.take(TF)
    zch = abc.take(TF)
    szt = abc.take(TF)
    wotk = [abc.take(NKT_DM * 128) for _ in range(2)]

    def walk(w: Walk):
        regs = {}
        if w.E == "gp":
            regs["y"] = nc.gpsimd.alloc_register("r_yoff")
            regs["t"] = nc.gpsimd.alloc_register("r_tmp")

        w.dma_sy(identt[:, :], ident)
        w.dma_sy(onesr[:, :], ones_row)
        w.dma_sy(cinfot[:, :], cinfo)
        w.dma_sy(bxc[:, :], b_x)
        w.dma_sy(rbc[:, :], r_b)
        w.dma_sy(binu_c[:, :], b_in_u)
        w.dma_sy(binz_c[:, :], b_in_z)
        w.dma_sy(bout_c[:, :], b_out_c)
        w.dma_sy(bdt_col[:, :], b_dt_sl)
        w.dma_sy(D_col[:, :], D_sl)
        for dtl in range(NDT):
            w.dma_sy(A_cols[:, dtl * DST:(dtl + 1) * DST],
                     A_log_sl[dtl * 128:(dtl + 1) * 128, :])
        ld_pre = w.snap("LD")
        w.wait("gp", "LD")
        if w.E == "gp":
            nc.gpsimd.reg_load(regs["y"], cinfot[0:1, 1:2])
        w.gp_(lambda g: g.nop())
        w.wait("se", "LD")
        if w.E == "se":
            regs["su"] = nc.scalar.alloc_register("r_su")
            nc.scalar.reg_load(regs["su"], cinfot[0:1, 0:1])
            regs["sU0"] = nc.scalar.alloc_register("r_sU0")
            nc.scalar.reg_alu(regs["sU0"], regs["su"], u_full[0].offset, ALU.add)
        w.se_(lambda s: s.nop())
        w.wait("ve", "LD")
        if w.E == "ve":
            regs["vu"] = nc.vector.alloc_register("r_vu")
            nc.vector.reg_load(regs["vu"], cinfot[0:1, 0:1])
            regs["vU0"] = nc.vector.alloc_register("r_vU0")
            nc.vector.reg_alu(regs["vU0"], regs["vu"], u_full[0].offset, ALU.add)
        w.ve_(lambda v: v.nop())


        w.wait("ve", "LD")
        w.ve_(lambda v: v.memset(onesc_bf[:, :], 1.0))
        w.ve_(lambda v: v.memset(cst[:, 0:1], LN_EPS))
        w.ve_(lambda v: v.memset(cst[:, 1:2], 1.0))
        w.ve_(lambda v: v.memset(hcar[:, :], 0.0))
        w.wait("se", "LD")
        w.se_(lambda s: s.activation(out=A_cols[:, :], in_=A_cols[:, :], func=AF.Exp))
        w.wait("ve", "SE")
        w.ve_(lambda v: v.tensor_scalar_mul(A_cols[:, :], A_cols[:, :], -1.0))

        def cast_cache(src_ap, dst_ap, numel):
            cols = numel // 128
            CC = 8192
            pos = 0
            sf = src_ap.rearrange("a b -> (a b)").rearrange("(p c) -> p c", p=128)
            df = dst_ap.rearrange("a b -> (a b)").rearrange("(p c) -> p c", p=128)
            while pos < cols:
                n = min(CC, cols - pos)
                w.wait("sy", "SE")
                w.dma_sy(cast_f[:, 0:n], sf[:, pos:pos + n])
                w.wait("se", "LD")
                w.wait("se", "ST")
                w.se_(lambda s, n=n: s.copy(cast_b[:, 0:n], cast_f[:, 0:n]))
                w.wait("se", "SE")
                w.dma_se(df[:, pos:pos + n], cast_b[:, 0:n])
                pos += n

        cast_cache(W_in, win_bf, DM * 2 * DI)
        cast_cache(gate_wT, gate_bf, NE * DM * HE)
        cast_cache(up_wT, up_bf, NE * DM * HE)
        cast_cache(down_eT, down_bf, NE * HE * DM)
        cast_cache(sg_wT, sg_bf, DM * HSH)
        cast_cache(su_wT, su_bf, DM * HSH)
        cast_cache(sd_wT, sd_bf, HSH * DM)
        cast_cache(W_out, wout_bf, DI * DM)
        cast_cache(r_wT, rw_bf, DM * NE)
        for kt in range(NKT_DI):
            w.wait("sy", "SE")
            w.dma_sy(cast_f[:, 0:2 * DST], W_x[kt * 128:(kt + 1) * 128, :])
            w.wait("se", "LD")
            w.se_(lambda s, kt=kt: s.copy(
                wx_bf[:, kt * 2 * DST:(kt + 1) * 2 * DST], cast_f[:, 0:2 * DST]))
        st_pre = w.snap("ST")
        se_pre = w.snap("SE")

        def ln_norm(src_f32, src_bf):
            w.wait("se", "SE")
            w.wait("se", "TE")
            w.se_(lambda s: s.activation(out=xsqbf[:, :], in_=src_bf[:, :],
                                         func=AF.Square))
            se_sq = w.snap("SE")
            w.wait("te", "SE", se_sq)
            w.wait("te", "VE")
            for kt in range(NKT_DM):
                w.te_(lambda t, kt=kt: t.matmul(
                    pE[0:1, 0:TF], onesc_bf[:, :], src_bf[:, kt * TF:(kt + 1) * TF],
                    start=(kt == 0), stop=(kt == NKT_DM - 1)))
            for kt in range(NKT_DM):
                w.te_(lambda t, kt=kt: t.matmul(
                    pE[32:33, 0:TF], onesc_bf[:, :], xsqbf[:, kt * TF:(kt + 1) * TF],
                    start=(kt == 0), stop=(kt == NKT_DM - 1)))
            te_stats = w.snap("TE")
            w.wait("ve", "TE", te_stats)
            w.ve_(lambda v: v.tensor_scalar_mul(mrow[0:1, :], pE[0:1, 0:TF], 1.0 / DM))
            w.wait("ve", "VE")
            w.ve_(lambda v: v.tensor_scalar_mul(vrow[0:1, :], pE[32:33, 0:TF], 1.0 / DM))
            w.wait("ve", "VE")
            w.ve_(lambda v: v.tensor_mul(m2row[0:1, :], mrow[0:1, :], mrow[0:1, :]))
            w.wait("ve", "VE")
            w.ve_(lambda v: v.tensor_sub(vrow[0:1, :], vrow[0:1, :], m2row[0:1, :]))
            ve_var = w.snap("VE")
            w.wait("se", "VE", ve_var)
            w.se_(lambda s: s.activation(out=rstdr[0:1, :], in_=vrow[0:1, :],
                                         func=AF.Ln, bias=cst[0:1, 0:1]))
            w.wait("se", "SE")
            w.se_(lambda s: s.activation(out=rstdr[0:1, :], in_=rstdr[0:1, :],
                                         func=AF.Exp, scale=-0.5))
            se_rstd = w.snap("SE")
            w.wait("te", "SE", se_rstd)
            w.te_(lambda t: t.matmul(pF[:, 0:TF], onesr[:, :], mrow[0:1, :],
                                     start=True, stop=True))
            w.te_(lambda t: t.matmul(pE[:, 0:TF], onesr[:, :], rstdr[0:1, :],
                                     start=True, stop=True))
            te_bc = w.snap("TE")
            w.wait("se", "TE", te_bc)
            w.se_(lambda s: s.copy(mrep[:, :], pF[:, 0:TF]))
            w.wait("se", "SE")
            w.se_(lambda s: s.copy(rrep[:, :], pE[:, 0:TF]))
            se_bc = w.snap("SE")
            w.wait("ve", "SE", se_bc)
            w.wait("ve", "TE")
            w.wait("ve", "ST")
            for kt in range(NKT_DM):
                w.ve_(lambda v, kt=kt: v.tensor_sub(
                    tmp8[:, kt * TF:(kt + 1) * TF],
                    src_f32[:, kt * TF:(kt + 1) * TF], mrep[:, :]))
            w.wait("ve", "VE")
            for kt in range(NKT_DM):
                w.ve_(lambda v, kt=kt: v.tensor_mul(
                    xnbf[:, kt * TF:(kt + 1) * TF],
                    tmp8[:, kt * TF:(kt + 1) * TF], rrep[:, :]))
            return w.snap("VE")

        def ffn(resid_src, resid_dst, ci):
            w.wait("sy", "TE")
            w.dma_sy(rwt[:, :].rearrange("p (kt e) -> p kt e", kt=NKT_DM),
                     rw_bf.rearrange("(kt p) e -> p kt e", p=128), stream="RW")
            w.wait("te", "RW")
            w.wait("te", "SE")
            w.wait("te", "VE")
            for kt in range(NKT_DM):
                w.te_(lambda t, kt=kt: t.matmul(
                    pE[0:NE, 0:TF], rwt[:, kt * NE:(kt + 1) * NE],
                    xnbf[:, kt * TF:(kt + 1) * TF],
                    start=(kt == 0), stop=(kt == NKT_DM - 1)))
            te_rt = w.snap("TE")
            w.wait("se", "TE", te_rt)
            w.se_(lambda s: s.activation(out=lgsb[0:NE, 0:TF], in_=pE[0:NE, 0:TF],
                                         func=AF.Identity, bias=rbc[:, :]))
            for tb in range(NTT):
                pt = pE if tb % 2 == 0 else pF
                w.wait("te", "SE")
                w.te_(lambda t, tb=tb, pt=pt: t.transpose(
                    pt[:, 0:NE], lgsb[0:NE, tb * 128:(tb + 1) * 128],
                    identt[0:NE, 0:NE]))
                w.wait("se", "TE")
                w.se_(lambda s, tb=tb, pt=pt: s.copy(
                    lgT[:, tb * NE:(tb + 1) * NE], pt[:, 0:NE]))
            se_lgT = w.snap("SE")

            for tb in range(NTT):
                blk = lgT[:, tb * NE:(tb + 1) * NE]
                w.wait("ve", "SE", se_lgT)
                w.wait("ve", "TE")
                w.ve_(lambda v: v.tensor_copy(wrk[:, :], blk))
                for k in range(4):
                    w.wait("ve", "VE")
                    w.ve_(lambda v, k=k: v.reduce_max(mks[:, k:k + 1], wrk[:, :], AX.X))
                    w.wait("ve", "VE")
                    w.ve_(lambda v, k=k: v.tensor_scalar(
                        masks[:, k * NE:(k + 1) * NE], wrk[:, :], mks[:, k:k + 1],
                        None, ALU.is_equal))
                    if k < 3:
                        w.wait("ve", "VE")
                        w.ve_(lambda v, k=k: v.scalar_tensor_tensor(
                            wrk[:, :], masks[:, k * NE:(k + 1) * NE], -BIGNEG,
                            wrk[:, :], ALU.mult, ALU.add))
                w.wait("ve", "VE")
                w.ve_(lambda v: v.tensor_scalar_mul(nm1[:, 0:1], mks[:, 0:1], -1.0))
                ve_nm1 = w.snap("VE")
                w.wait("se", "VE", ve_nm1)
                w.se_(lambda s: s.activation(out=e4[:, :], in_=mks[:, :],
                                             func=AF.Exp, bias=nm1[:, 0:1]))
                se_e4 = w.snap("SE")
                w.wait("ve", "SE", se_e4)
                w.ve_(lambda v: v.reduce_sum(s4[:, 0:1], e4[:, :], AX.X))
                w.wait("ve", "VE")
                w.ve_(lambda v: v.reciprocal(r4[:, 0:1], s4[:, 0:1]))
                w.wait("ve", "VE")
                w.ve_(lambda v: v.tensor_scalar_mul(wv4[:, :], e4[:, :], r4[:, 0:1]))
                w.wait("ve", "VE")
                w.ve_(lambda v: v.tensor_scalar(
                    w32[:, :], masks[:, 0:NE], wv4[:, 0:1], None, ALU.mult))
                for k in range(1, 4):
                    w.wait("ve", "VE")
                    w.ve_(lambda v, k=k: v.scalar_tensor_tensor(
                        w32[:, :], masks[:, k * NE:(k + 1) * NE], wv4[:, k:k + 1],
                        w32[:, :], ALU.mult, ALU.add))
                ve_w32 = w.snap("VE")
                pt = pE if tb % 2 == 0 else pF
                w.wait("te", "VE", ve_w32)
                w.wait("te", "SE")
                w.te_(lambda t, pt=pt: t.transpose(pt[0:NE, 0:128], w32[:, :],
                                                   identt[:, :]))
                w.wait("se", "TE")
                w.se_(lambda s, tb=tb, pt=pt: s.copy(
                    wTsb[0:NE, tb * 128:(tb + 1) * 128], pt[0:NE, 0:128]))
            se_wT = w.snap("SE")
            w.wait("se", "SE", se_wT)
            w.wait("se", "ST")
            w.se_(lambda s: s.copy(wTsb_bf[0:NE, 0:TF], wTsb[0:NE, 0:TF]))
            w.wait("se", "SE")
            w.dma_se(wT_d[:, ci * TF:(ci + 1) * TF], wTsb_bf[0:NE, 0:TF])
            st_wT = w.snap("ST")

            def load_wrep(g4):
                w.wait("sy", "ST", st_wT)
                w.wait("sy", "VE")
                src = wT_d[g4 * 4:(g4 + 1) * 4, ci * TF:(ci + 1) * TF]
                src = src.rearrange("e (o c) -> e o c", o=1).broadcast_to(
                    [4, 128, TF]).transpose([1, 0, 2])
                w.dma_sy(wrep[:, :].rearrange("p (e c) -> p e c", e=4), src,
                         stream="LDR")
                return w.snap("LDR")

            def load_wg4(g4, te_free):
                gb = g4 % 2
                w.wait("sy", "TE", te_free)
                srcg = gate_bf[g4 * 4 * DM:(g4 + 1) * 4 * DM, :].rearrange(
                    "(e kt p) c -> p e kt c", e=4, p=128)
                w.dma_sy(wg4[gb][:, :].rearrange("p (e kt c) -> p e kt c",
                                                 e=4, kt=NKT_DM), srcg,
                         stream=f"W{gb}")
                srcu = up_bf[g4 * 4 * DM:(g4 + 1) * 4 * DM, :].rearrange(
                    "(e kt p) c -> p e kt c", e=4, p=128)
                w.dma_sy(wu4[gb][:, :].rearrange("p (e kt c) -> p e kt c",
                                                 e=4, kt=NKT_DM), srcu,
                         stream=f"W{gb}")
                return w.snap(f"W{gb}"), gb

            ld_wrep = load_wrep(0)
            ld_wg, gbuf = load_wg4(0, 0)
            gbuf_of = {0: 0}
            te_grp_hist = {}
            sg_hist = [0, 0]
            h1_hist = [0, 0]
            h2_hist = [0, 0]
            for e in range(NE):
                if e % 4 == 0 and e > 0:
                    ld_wrep = load_wrep(e // 4)
                if e % 4 == 0 and e > 0:
                    g4 = e // 4
                    te_free = te_grp_hist.get(g4 - 2, 0)
                    ld_wg, gbuf = load_wg4(g4, te_free)
                    gbuf_of[g4] = gbuf
                eo = (e % 4) * NKT_DM * 128
                pg = pA if e % 2 == 0 else pB
                pu = pC if e % 2 == 0 else pD
                sgb = sgbf[:, (e % 2) * TF:((e % 2) + 1) * TF]
                w.wait("te", "SE", sg_hist[0])
                w.wait("te", "VE", h1_hist[0])
                gb = gbuf_of[e // 4]
                w.wait("te", f"W{gb}")
                for kt in range(NKT_DM):
                    w.te_(lambda t, kt=kt, eo=eo, pg=pg, gb=gb: t.matmul(
                        pg[:, 0:TF], wg4[gb][:, eo + kt * 128: eo + (kt + 1) * 128],
                        xnbf[:, kt * TF:(kt + 1) * TF],
                        start=(kt == 0), stop=(kt == NKT_DM - 1)))
                te_g = w.snap("TE")
                for kt in range(NKT_DM):
                    w.te_(lambda t, kt=kt, eo=eo, pu=pu, gb=gb: t.matmul(
                        pu[:, 0:TF], wu4[gb][:, eo + kt * 128: eo + (kt + 1) * 128],
                        xnbf[:, kt * TF:(kt + 1) * TF],
                        start=(kt == 0), stop=(kt == NKT_DM - 1)))
                te_u = w.snap("TE")
                if e % 4 == 3:
                    te_grp_hist[e // 4] = te_u
                w.wait("se", "TE", te_g)
                w.wait("se", "VE", h1_hist[1])
                w.se_(lambda s, pg=pg, sgb=sgb: s.activation(
                    out=sgb, in_=pg[:, 0:TF], func=AF.Silu))
                se_sg = w.snap("SE")
                w.wait("ve", "TE", te_u)
                w.wait("ve", "SE", se_sg)
                w.wait("ve", "LDR")
                h1b = h1t[:, (e % 2) * TF:((e % 2) + 1) * TF]
                w.ve_(lambda v, pu=pu, sgb=sgb, h1b=h1b: v.tensor_mul(
                    h1b, sgb, pu[:, 0:TF]))
                ve_h1 = w.snap("VE")
                w.wait("ve", "VE", ve_h1)
                w.ve_(lambda v, e=e, h1b=h1b: v.tensor_mul(
                    hcache[:, e * TF:(e + 1) * TF], h1b,
                    wrep[:, (e % 4) * TF:((e % 4) + 1) * TF]))
                sg_hist = [sg_hist[1], se_sg]
                h1_hist = [h1_hist[1], ve_h1]
                h2_hist = [h2_hist[1], w.snap("VE")]

            sh_sg_hist = [w.snap("SE"), w.snap("SE")]
            sh_h_hist = [h2_hist[0], h2_hist[1]]
            for ms in range(HSH // 128):
                buf = ms % 2
                pg = pA if buf == 0 else pB
                pu = pC if buf == 0 else pD
                sgb = sgbf[:, buf * TF:(buf + 1) * TF]
                w.wait("sy", "TE")
                w.dma_sy(wshg[buf][:, :].rearrange("p (kt c) -> p kt c", kt=NKT_DM),
                         sg_bf[:, ms * 128:(ms + 1) * 128].rearrange(
                             "(kt p) c -> p kt c", p=128), stream=f"S{buf}")
                w.dma_sy(wshu[buf][:, :].rearrange("p (kt c) -> p kt c", kt=NKT_DM),
                         su_bf[:, ms * 128:(ms + 1) * 128].rearrange(
                             "(kt p) c -> p kt c", p=128), stream=f"S{buf}")
                w.wait("te", f"S{buf}")
                w.wait("te", "SE", sh_sg_hist[0] if ms >= 2 else w.snap("SE"))
                w.wait("te", "VE", sh_h_hist[0] if ms >= 2 else w.snap("VE"))
                for kt in range(NKT_DM):
                    w.te_(lambda t, kt=kt, buf=buf, pg=pg: t.matmul(
                        pg[:, 0:TF], wshg[buf][:, kt * 128:(kt + 1) * 128],
                        xnbf[:, kt * TF:(kt + 1) * TF],
                        start=(kt == 0), stop=(kt == NKT_DM - 1)))
                te_g = w.snap("TE")
                for kt in range(NKT_DM):
                    w.te_(lambda t, kt=kt, buf=buf, pu=pu: t.matmul(
                        pu[:, 0:TF], wshu[buf][:, kt * 128:(kt + 1) * 128],
                        xnbf[:, kt * TF:(kt + 1) * TF],
                        start=(kt == 0), stop=(kt == NKT_DM - 1)))
                te_u = w.snap("TE")
                w.wait("se", "TE", te_g)
                w.wait("se", "VE", sh_h_hist[1] if ms >= 1 else w.snap("VE"))
                w.se_(lambda s, pg=pg, sgb=sgb: s.activation(
                    out=sgb, in_=pg[:, 0:TF], func=AF.Silu))
                se_sg = w.snap("SE")
                w.wait("ve", "TE", te_u)
                w.wait("ve", "SE", se_sg)
                w.ve_(lambda v, ms=ms, pu=pu, sgb=sgb: v.tensor_mul(
                    hsc[:, ms * TF:(ms + 1) * TF], sgb, pu[:, 0:TF]))
                sh_sg_hist = [sh_sg_hist[1], se_sg]
                sh_h_hist = [sh_h_hist[1], w.snap("VE")]

            ve_all_h = w.snap("VE")
            dn_hist = [0, 0]
            dn_te_hist = [0, 0]
            for m in range(DM // 128):
                buf = 0
                pd = pG if m % 2 == 0 else pH
                w.wait("sy", "TE", dn_te_hist[1] if m >= 1 else 0)
                w.dma_sy(wdn[buf][:, :].rearrange("p (e c) -> p e c", e=NE),
                         down_bf[:, m * 128:(m + 1) * 128].rearrange(
                             "(e p) c -> p e c", p=128), stream="DN")
                sdb = m % 2
                w.wait("sy", "TE", dn_te_hist[1] if m >= 1 else 0)
                w.dma_sy(wshd[sdb][:, :].rearrange("p (ks c) -> p ks c",
                                                   ks=HSH // 128),
                         sd_bf[:, m * 128:(m + 1) * 128].rearrange(
                             "(ks p) c -> p ks c", p=128), stream=f"S{sdb}")
                w.wait("te", "DN")
                w.wait("te", f"S{sdb}")
                w.wait("te", "VE", max(ve_all_h, dn_hist[0]))
                for e in range(NE):
                    w.te_(lambda t, e=e, buf=buf, pd=pd: t.matmul(
                        pd[:, 0:TF], wdn[buf][:, e * 128:(e + 1) * 128],
                        hcache[:, e * TF:(e + 1) * TF],
                        start=(e == 0), stop=False))
                for ks in range(HSH // 128):
                    w.te_(lambda t, ks=ks, pd=pd, buf=m % 2: t.matmul(
                        pd[:, 0:TF], wshd[buf][:, ks * 128:(ks + 1) * 128],
                        hsc[:, ks * TF:(ks + 1) * TF],
                        start=False, stop=(ks == HSH // 128 - 1)))
                te_dn = w.snap("TE")
                dn_te_hist = [dn_te_hist[1], te_dn]
                w.wait("ve", "TE", te_dn)
                w.ve_(lambda v, m=m, pd=pd: v.scalar_tensor_tensor(
                    resid_dst[:, m * TF:(m + 1) * TF], pd[:, 0:TF], 0.5,
                    resid_src[:, m * TF:(m + 1) * TF], ALU.mult, ALU.add))
                dn_hist = [dn_hist[1], w.snap("VE")]
            return w.snap("VE")

        w.wait("sy", "SE", se_pre)
        w.wait("sy", "ST", st_pre)
        for ci in range(NCHF):
            w.wait("sy", "VE")
            w.wait("sy", "SE")
            w.wait("sy", "TE")
            for tt in range(NTT):
                w.dma_sy(xtm[:, tt * DM:(tt + 1) * DM],
                         x_my[ci * TF + tt * 128: ci * TF + (tt + 1) * 128, :],
                         stream="LDX")
            for dk in range(NKT_DM):
                pt = pE if dk % 2 == 0 else pF
                w.wait("te", "LDX")
                w.wait("te", "SE")
                for tt in range(NTT):
                    w.te_(lambda t, dk=dk, tt=tt, pt=pt: t.transpose(
                        pt[:, tt * 128:(tt + 1) * 128],
                        xtm[:, tt * DM + dk * 128: tt * DM + (dk + 1) * 128],
                        identt[:, :]))
                te_tr = w.snap("TE")
                w.wait("se", "TE", te_tr)
                w.se_(lambda s, dk=dk, pt=pt: s.copy(
                    x_res[:, dk * TF:(dk + 1) * TF], pt[:, 0:TF]))
            w.wait("se", "SE")
            w.se_(lambda s: s.copy(xbf[:, :], x_res[:, :]))

            ln_norm(x_res, xbf)
            ffn(x_res, x1c, ci)
            ve_x1 = w.snap("VE")

            w.wait("se", "VE", ve_x1)
            for dk in range(NKT_DM):
                w.dma_se(x1_d[dk * 128:(dk + 1) * 128, ci * TF:(ci + 1) * TF],
                         x1c[:, dk * TF:(dk + 1) * TF])
            w.se_(lambda s: s.copy(xbf[:, :], x1c[:, :]))
            ln_norm(x1c, xbf)
            wi_hist = [0, 0]
            wi_te_hist = [0, 0]
            ve_xn2 = w.snap("VE")
            for m in range(2 * DI // 128):
                buf = m % 2
                pw = pA if buf == 0 else pB
                w.wait("sy", "TE", wi_te_hist[0] if m >= 2 else 0)
                w.dma_sy(wint[buf][:, :].rearrange("p (kt c) -> p kt c", kt=NKT_DM),
                         win_bf[:, m * 128:(m + 1) * 128].rearrange(
                             "(kt p) c -> p kt c", p=128), stream=f"I{buf}")
                w.wait("te", f"I{buf}")
                w.wait("te", "SE", wi_hist[0] if m >= 2 else w.snap("SE"))
                w.wait("te", "VE", ve_xn2)
                for kt in range(NKT_DM):
                    w.te_(lambda t, kt=kt, buf=buf, pw=pw: t.matmul(
                        pw[:, 0:TF], wint[buf][:, kt * 128:(kt + 1) * 128],
                        xnbf[:, kt * TF:(kt + 1) * TF],
                        start=(kt == 0), stop=(kt == NKT_DM - 1)))
                te_wi = w.snap("TE")
                wi_te_hist = [wi_te_hist[1], te_wi]
                w.wait("se", "TE", te_wi)
                w.wait("se", "ST")
                if m < DI // 128:
                    w.se_(lambda s, m=m, pw=pw: s.activation(
                        out=ubf_sb[:, :], in_=pw[:, 0:TF], func=AF.Silu,
                        bias=binu_c[:, m:m + 1]))
                    w.wait("se", "SE")
                    kag = (ci * TF) // (TPC // 4)
                    off = (ci * TF) % (TPC // 4)
                    w.dma_se(u_in2[kag][m * 128:(m + 1) * 128, off:off + TF],
                             ubf_sb[:, :])
                    wi_hist = [wi_hist[1], w.snap("SE")]
                else:
                    mz = m - DI // 128
                    w.se_(lambda s, pw=pw: s.copy(zbf_sb[:, :], pw[:, 0:TF]))
                    w.wait("se", "SE")
                    w.dma_se(z_d[mz * 128:(mz + 1) * 128, ci * TF:(ci + 1) * TF],
                             zbf_sb[:, :])
                    wi_hist = [wi_hist[1], w.snap("SE")]
        st_ab = w.snap("ST")

        CQ = TPC // 4
        for kag in range(4):
            w.wait("gp", "ST", st_ab)
            w.ag([u_in2[kag].opt()], [u_ag2[kag].opt()],
                 [[0, 1, 2, 3], [4, 5, 6, 7]])
        w.wait("gp", "AG")
        w.gp_(lambda g: g.nop())
        gp_agu = w.snap("GP")

        for kt in range(NKT_DI):
            w.wait("sy", "SE")
            w.wait("sy", "VE")
            w.wait("sy", "TE")
            w.dma_sy(cast_f[:, 0:DG], W_dt_sl[kt * 128:(kt + 1) * 128, :])
            w.wait("se", "LD")
            w.se_(lambda s, kt=kt: s.copy(
                wdt_bf[:, kt * DG:(kt + 1) * DG], cast_f[:, 0:DG]))
        w.wait("se", "SE")
        se_wdt = w.snap("SE")

        hist_te_ufull = [0, 0]
        hist_se_ugcast = [0, 0]
        hist_ve_ugf = [0, 0]
        hist_se_sp = [se_wdt, se_wdt]
        hist_ve_sp = [[0, 0], [0, 0]]
        hist_se_bc = se_wdt
        red_hist = [0, 0]
        bc_hist = [0, 0]
        se_cast_hist = 0
        for scp in range(NCHS // 2):
          for cb2 in range(2):
            sc = 2 * scp + cb2
            t0 = sc * TS
            q_src = sc % 4
            jst = sc // 4
            kag = (jst * TS) // (TPC // 4)
            cofs = (jst * TS) % (TPC // 4)
            w.wait("sy", "GP", gp_agu)
            w.wait("sy", "TE", hist_te_ufull[cb2])
            w.wait("sy", "SE", hist_se_ugcast[0])
            w.wait("sy", "VE", hist_ve_ugf[cb2])
            src_ = u_ag2[kag][q_src * DI:(q_src + 1) * DI,
                              cofs:cofs + TS].rearrange(
                "(kt p) c -> p kt c", p=128)
            w.dma_sy(u_full[cb2][:, :].rearrange("p (kt c) -> p kt c", kt=NKT_DI),
                     src_, stream=f"U{cb2}")
          out_ap = bass.AP(af_t, ug_f[0].offset,
                           [[F32_COLS, 128], [ug_f[1].offset - ug_f[0].offset, 2],
                            [1, NDT * TS]])
          if scp == 0:
              w.wait("ve", "U0")
              w.wait("ve", "U1")
              w.wait("ve", "SE")
              if w.E == "ve":
                  nc.vector.tensor_copy(
                      out_ap,
                      bass.AP(ab_t, regs["vU0"],
                              [[BF_COLS, 128],
                               [u_full[1].offset - u_full[0].offset, 2],
                               [1, NDT * TS]]))
              w.ve_(lambda v: v.nop())
              se_ug = 0
              hist_se_ugcast[0] = 0
          else:
              w.wait("se", "U0")
              w.wait("se", "U1")
              w.wait("se", "VE", max(hist_ve_ugf[0], hist_ve_ugf[1]))
              if w.E == "se":
                  nc.scalar.copy(
                      out_ap,
                      bass.AP(ab_t, regs["sU0"],
                              [[BF_COLS, 128],
                               [u_full[1].offset - u_full[0].offset, 2],
                               [1, NDT * TS]]))
              w.se_(lambda s: s.nop())
              se_ug = w.snap("SE")
              hist_se_ugcast[0] = se_ug
          for cb in range(2):
            sc = 2 * scp + cb
            t0 = sc * TS
            for mt in range(NDT):
                pw = pA if mt % 2 == 0 else pB
                w.wait("te", f"U{cb}")
                w.wait("te", "SE", hist_se_sp[mt % 2])
                for kt in range(NKT_DI):
                    w.te_(lambda t, kt=kt, mt=mt, pw=pw, cb=cb: t.matmul(
                        pw[:, 0:TS],
                        wdt_bf[:, (kt * NDT + mt) * 128:(kt * NDT + mt) * 128 + 128],
                        u_full[cb][:, kt * TS:(kt + 1) * TS],
                        start=(kt == 0), stop=(kt == NKT_DI - 1)))
                te_dt = w.snap("TE")
                spb = spA[cb][mt % 2]
                w.wait("se", "TE", te_dt)
                w.wait("se", "VE", hist_ve_ugf[cb])
                w.wait("se", "VE", hist_ve_sp[cb][mt % 2])
                w.se_(lambda s, mt=mt, pw=pw, spb=spb: s.activation(
                    out=spb[:, 0:TS], in_=pw[:, 0:TS],
                    func=AF.Abs, bias=bdt_col[:, mt:mt + 1]))
                w.wait("se", "SE")
                w.se_(lambda s, spb=spb: s.activation(
                    out=spb[:, 0:TS], in_=spb[:, 0:TS],
                    func=AF.Exp, scale=-1.0))
                w.wait("se", "SE")
                w.se_(lambda s, spb=spb: s.activation(
                    out=spb[:, 0:TS], in_=spb[:, 0:TS],
                    func=AF.Ln, bias=cst[:, 1:2]))
                w.wait("se", "SE")
                w.se_(lambda s, mt=mt, pw=pw, cb=cb: s.activation(
                    out=dt_f[cb][:, mt * TS:(mt + 1) * TS], in_=pw[:, 0:TS],
                    func=AF.Relu, bias=bdt_col[:, mt:mt + 1]))
                se_sp = w.snap("SE")
                hist_se_sp[mt % 2] = se_sp
                w.wait("ve", "SE", se_sp)
                w.ve_(lambda v, mt=mt, cb=cb, spb=spb: v.tensor_add(
                    dt_f[cb][:, mt * TS:(mt + 1) * TS],
                    dt_f[cb][:, mt * TS:(mt + 1) * TS], spb[:, 0:TS]))
                hist_ve_sp[cb][mt % 2] = w.snap("VE")
            ve_dt = w.snap("VE")

            w.wait("te", "SE", hist_se_bc)
            for kt in range(NKT_DI):
                w.te_(lambda t, kt=kt, cb=cb: t.matmul(
                    pC[0:2 * DST, 0:TS], wx_bf[:, kt * 2 * DST:(kt + 1) * 2 * DST],
                    u_full[cb][:, kt * TS:(kt + 1) * TS],
                    start=(kt == 0), stop=(kt == NKT_DI - 1)))
            te_x = w.snap("TE")
            hist_te_ufull[cb] = te_x
            w.wait("se", "TE", te_x)
            w.wait("se", "ST")
            w.se_(lambda s, cb=cb: s.activation(out=bc_sb[cb][0:2 * DST, 0:TS],
                                                in_=pC[0:2 * DST, 0:TS],
                                                func=AF.Identity, bias=bxc[:, :]))
            hist_se_bc = w.snap("SE")
            w.wait("se", "SE")
            w.dma_se(bcT_d[:, t0:t0 + TS], bc_sb[cb][0:2 * DST, 0:TS])
            st_bc = w.snap("ST")

            w.wait("ve", "SE", se_ug)
            w.wait("ve", "VE", ve_dt)
            for dtl in range(NDT):
                w.ve_(lambda v, dtl=dtl, cb=cb: v.tensor_mul(
                    dtu_b[cb][:, dtl * TS:(dtl + 1) * TS],
                    dt_f[cb][:, dtl * TS:(dtl + 1) * TS],
                    ug_f[cb][:, dtl * TS:(dtl + 1) * TS]))
            ve_dtu = w.snap("VE")

            for sb in range(NSB):
                bb = sb % 2
                w.wait("sy", "ST", st_bc)
                w.wait("sy", "VE", bc_hist[bb])
                srcB = bcT_d[sb * SBK:(sb + 1) * SBK, t0:t0 + TS]
                srcB = srcB.rearrange("s (o c) -> s o c", o=1).broadcast_to(
                    [SBK, 128, TS]).transpose([1, 0, 2])
                w.dma_sy(brep[bb][:, :].rearrange("p (s c) -> p s c", s=SBK), srcB,
                         stream=f"B{bb}")
                srcC = bcT_d[DST + sb * SBK: DST + (sb + 1) * SBK, t0:t0 + TS]
                srcC = srcC.rearrange("s (o c) -> s o c", o=1).broadcast_to(
                    [SBK, 128, TS]).transpose([1, 0, 2])
                w.dma_sy(crep[bb][:, :].rearrange("p (s c) -> p s c", s=SBK), srcC,
                         stream=f"B{bb}")

                for dtl in range(NDT):
                    blk = sb * NDT + dtl
                    pb = blk % 2
                    dA_sl = dA_bufs[pb]
                    w.wait("se", "VE", red_hist[pb])
                    for j in range(SBK):
                        s_idx = sb * SBK + j
                        w.se_(lambda s, j=j, dtl=dtl, s_idx=s_idx, dA_sl=dA_sl,
                              cb=cb: s.activation(
                            out=dA_sl[:, j * TS:(j + 1) * TS],
                            in_=dt_f[cb][:, dtl * TS:(dtl + 1) * TS], func=AF.Exp,
                            scale=A_cols[:, dtl * DST + s_idx: dtl * DST + s_idx + 1]))
                    se_dA = w.snap("SE")
                    w.wait("ve", f"B{bb}")
                    w.wait("ve", "VE", ve_dtu)
                    for j in range(SBK):
                        w.ve_(lambda v, j=j, dtl=dtl, bb=bb, cb=cb: v.tensor_mul(
                            w_sl[:, j * TS:(j + 1) * TS],
                            dtu_b[cb][:, dtl * TS:(dtl + 1) * TS],
                            brep[bb][:, j * TS:(j + 1) * TS]))
                    ve_w = w.snap("VE")
                    w.wait("ve", "VE", ve_w)
                    w.wait("ve", "SE", se_dA)
                    for j in range(SBK):
                        col = dtl * DST + sb * SBK + j
                        w.ve_(lambda v, j=j, col=col, dA_sl=dA_sl:
                              v.tensor_tensor_scan(
                            h_sl[:, j * TS:(j + 1) * TS],
                            dA_sl[:, j * TS:(j + 1) * TS],
                            w_sl[:, j * TS:(j + 1) * TS],
                            hcar[:, col:col + 1], ALU.mult, ALU.add))
                    ve_scan = w.snap("VE")
                    w.wait("ve", "VE", ve_scan)
                    for j in range(SBK):
                        w.ve_(lambda v, j=j, dA_sl=dA_sl, bb=bb: v.tensor_mul(
                            dA_sl[:, j * TS:(j + 1) * TS],
                            h_sl[:, j * TS:(j + 1) * TS],
                            crep[bb][:, j * TS:(j + 1) * TS]))
                    ve_hC = w.snap("VE")
                    w.wait("ve", "VE", ve_hC)
                    w.wait("ve", "SE", se_cast_hist)
                    hhview = dA_sl[:, :].rearrange("p (s c) -> p c s", s=SBK)
                    if sb == 0:
                        w.ve_(lambda v, dtl=dtl: v.tensor_reduce(
                            yacc[:, dtl * TS:(dtl + 1) * TS], hhview, AX.X, ALU.add))
                    else:
                        w.ve_(lambda v: v.tensor_reduce(
                            yred[:, 0:TS], hhview, AX.X, ALU.add))
                        w.wait("ve", "VE")
                        w.ve_(lambda v, dtl=dtl: v.tensor_add(
                            yacc[:, dtl * TS:(dtl + 1) * TS],
                            yacc[:, dtl * TS:(dtl + 1) * TS], yred[:, 0:TS]))
                    ve_red = w.snap("VE")
                    red_hist[pb] = ve_red
                    w.wait("se", "VE", ve_red)
                    hl = h_sl[:, :].rearrange("p (s c) -> p s c", s=SBK)[:, :, TS - 1:TS]
                    hl = hl.rearrange("p s o -> p (s o)")
                    col0 = dtl * DST + sb * SBK
                    w.se_(lambda s, col0=col0, hl=hl: s.copy(
                        hcar[:, col0:col0 + SBK], hl))
                    if dtl == NDT - 1:
                        bc_hist[bb] = w.snap("VE")

            for dtl in range(NDT):
                w.wait("ve", "VE")
                w.ve_(lambda v, dtl=dtl, cb=cb: v.scalar_tensor_tensor(
                    yacc[:, dtl * TS:(dtl + 1) * TS],
                    ug_f[cb][:, dtl * TS:(dtl + 1) * TS], D_col[:, dtl:dtl + 1],
                    yacc[:, dtl * TS:(dtl + 1) * TS], ALU.mult, ALU.add))
            ve_y = w.snap("VE")
            hist_ve_ugf[cb] = ve_y
            w.wait("se", "VE", ve_y)
            w.wait("se", "ST")
            w.se_(lambda s: s.copy(ybf_s[:, :], yacc[:, 0:NDT * TS]))
            se_cast_hist = w.snap("SE")
            w.wait("se", "SE")
            mag = (sc * TS) // (T // 4)
            yofs = (sc * TS) % (T // 4)
            for dtl in range(NDT):
                w.dma_se(y_in2[mag][dtl * 128:(dtl + 1) * 128, yofs:yofs + TS],
                         ybf_s[:, dtl * TS:(dtl + 1) * TS])
        st_y = w.snap("ST")

        TQ = T // 4
        for mag in range(4):
            w.wait("gp", "ST", st_y)
            w.ag([y_in2[mag].opt()], [y_ag2[mag].opt()],
                 [[0, 1, 2, 3], [4, 5, 6, 7]])
            w.wait("gp", "AG")
            nown = TQ // (4 * TF)
            for i in range(nown):
                w.wait("gp", "GD")
                w.dma_gp_dyn(lambda g, mag=mag, i=i: (
                    g.reg_alu(regs["t"], regs["y"],
                              mag * DI * TQ + i * 4 * TF, ALU.add),
                    g.dma_start(
                        out=gbulk[:, 0:NKT_DI * TF].rearrange(
                            "p (rt c) -> p rt c", rt=NKT_DI),
                        in_=bass.AP(y_ag2.tensor, regs["t"],
                                    [[TQ, 128], [128 * TQ, NKT_DI],
                                     [1, TF]])))[1])
                w.wait("gp", "GD")
                ci_loc = nown * mag + i
                w.dma_gp_dyn(lambda g, ci_loc=ci_loc: g.dma_start(
                    out=y_gs_d[:, ci_loc * TF:(ci_loc + 1) * TF].rearrange(
                        "(rt p) c -> p rt c", p=128),
                    in_=gbulk[:, 0:NKT_DI * TF].rearrange(
                        "p (rt c) -> p rt c", rt=NKT_DI)))
        gd_ybulk = w.snap("GD")
        w.gp_(lambda g: g.nop())
        gp_agy = w.snap("GP")

        for ci in range(NCHF):
            w.wait("sy", "VE")
            w.wait("sy", "SE")
            w.wait("sy", "TE")
            w.dma_sy(x1c[:, :].rearrange("p (kt c) -> p kt c", kt=NKT_DM),
                     x1_d[:, ci * TF:(ci + 1) * TF].rearrange(
                         "(kt p) c -> p kt c", p=128), stream="LDX")
            wo_te_hist = [0, 0]
            for kt in range(NKT_DI):
                kb = kt % 2
                w.wait("sy", "GD", gd_ybulk)
                w.wait("sy", "VE")
                w.wait("sy", "TE", wo_te_hist[1] if kt >= 1 else 0)
                w.dma_sy(ymk[:, :],
                         y_gs_d[kt * 128:(kt + 1) * 128, ci * TF:(ci + 1) * TF],
                         stream="LDX")
                w.wait("sy", "SE")
                w.dma_sy(zch[:, :],
                         z_d[kt * 128:(kt + 1) * 128, ci * TF:(ci + 1) * TF],
                         stream="LDX")
                w.dma_sy(wotk[kb][:, :],
                         wout_bf[kt * 128:(kt + 1) * 128, :], stream=f"O{kb}")
                w.wait("se", "LDX")
                w.wait("se", "VE")
                w.se_(lambda s, kt=kt: s.activation(
                    out=szt[:, :], in_=zch[:, :], func=AF.Silu,
                    bias=binz_c[:, kt:kt + 1]))
                se_sz = w.snap("SE")
                w.wait("ve", "SE", se_sz)
                w.wait("ve", "LDX")
                w.wait("ve", "TE")
                w.ve_(lambda v: v.tensor_mul(ymk[:, :], ymk[:, :], szt[:, :]))
                ve_ymk = w.snap("VE")
                w.wait("te", "VE", ve_ymk)
                w.wait("te", f"O{kb}")
                if kt == 0:
                    w.wait("te", "SE")
                for m in range(DM // 128):
                    w.te_(lambda t, m=m, kt=kt, kb=kb: t.matmul(
                        pALL[m][:, 0:TF], wotk[kb][:, m * 128:(m + 1) * 128],
                        ymk[:, :],
                        start=(kt == 0), stop=(kt == NKT_DI - 1)))
                wo_te_hist = [wo_te_hist[1], w.snap("TE")]
            te_wo = w.snap("TE")
            w.wait("ve", "TE", te_wo)
            w.wait("ve", "LDX")
            for m in range(DM // 128):
                w.wait("ve", "VE")
                w.ve_(lambda v, m=m: v.scalar_tensor_tensor(
                    x1c[:, m * TF:(m + 1) * TF], pALL[m][:, 0:TF],
                    bout_c[:, m:m + 1], x1c[:, m * TF:(m + 1) * TF],
                    ALU.add, ALU.add))
            ve_x2 = w.snap("VE")

            w.wait("se", "VE", ve_x2)
            w.se_(lambda s: s.copy(xbf[:, :], x1c[:, :]))
            ln_norm(x1c, xbf)
            ffn(x1c, x_res, ci)
            ve_out = w.snap("VE")

            for tt in range(NTT):
                pt = pE if tt % 2 == 0 else pF
                for half in range(2):
                    w.wait("te", "VE", ve_out)
                    w.wait("te", "SE")
                    for dk4 in range(4):
                        dk = half * 4 + dk4
                        w.te_(lambda t, tt=tt, dk=dk, dk4=dk4, pt=pt: t.transpose(
                            pt[:, dk4 * 128:(dk4 + 1) * 128],
                            x_res[:, dk * TF + tt * 128: dk * TF + tt * 128 + 128],
                            identt[:, :]))
                    te_t = w.snap("TE")
                    w.wait("se", "TE", te_t)
                    w.se_(lambda s, tt=tt, half=half, pt=pt: s.copy(
                        xtm[:, tt * DM + half * 512: tt * DM + half * 512 + 512],
                        pt[:, 0:512]))
            se_otm = w.snap("SE")
            w.wait("se", "SE", se_otm)
            for tt in range(NTT):
                w.dma_se(out_my[ci * TF + tt * 128: ci * TF + (tt + 1) * 128, :],
                         xtm[:, tt * DM:(tt + 1) * DM])
        st_out = w.snap("ST")

        w.wait("gp", "ST", st_out)
        w.wait("sy", "ST", st_out)

    with nc.Block() as block:
        walks = {E: Walk(nc, E, sems) for E in Walk.ENGS}

        def mk(E):
            def f(eng):
                walk(walks[E])
            return f
        block.sync(mk("sy"))
        block.gpsimd(mk("gp"))
        block.vector(mk("ve"))
        block.scalar(mk("se"))
        block.tensor(mk("te"))
    ref = walks["sy"].c
    for E in Walk.ENGS:
        assert walks[E].c == ref, f"walk divergence {E}: {walks[E].c} vs {ref}"
    return nc


def make_in_maps(cfg, inputs):
    x = np.ascontiguousarray(inputs["L0_attn_x"])
    TPC = cfg.TPC
    gate_wT = np.ascontiguousarray(
        inputs["L0_ffn_gate_w"].transpose(0, 2, 1)).reshape(NE * DM, HE)
    up_wT = np.ascontiguousarray(
        inputs["L0_ffn_up_w"].transpose(0, 2, 1)).reshape(NE * DM, HE)
    down_eT = np.ascontiguousarray(
        inputs["L0_ffn_down_w"].transpose(0, 2, 1)).reshape(NE * HE, DM)
    b_in = inputs["L0_attn_b_in"]
    common = {
        "W_in": np.ascontiguousarray(inputs["L0_attn_W_in"]),
        "b_in_u": np.ascontiguousarray(b_in[:DI].reshape(DI // 128, 128).T),
        "b_in_z": np.ascontiguousarray(b_in[DI:].reshape(DI // 128, 128).T),
        "W_x": np.ascontiguousarray(inputs["L0_attn_W_x"]),
        "b_x": np.ascontiguousarray(inputs["L0_attn_b_x"].reshape(2 * DST, 1)),
        "W_out": np.ascontiguousarray(inputs["L0_attn_W_out"]),
        "b_out_c": np.ascontiguousarray(
            inputs["L0_attn_b_out"].reshape(DM // 128, 128).T),
        "r_wT": np.ascontiguousarray(inputs["L0_ffn_router_w"].T),
        "r_b": np.ascontiguousarray(inputs["L0_ffn_router_bias"].reshape(NE, 1)),
        "gate_wT": gate_wT,
        "up_wT": up_wT,
        "down_eT": down_eT,
        "sg_wT": np.ascontiguousarray(inputs["L0_ffn_sgate_w"].T),
        "su_wT": np.ascontiguousarray(inputs["L0_ffn_sup_w"].T),
        "sd_wT": np.ascontiguousarray(inputs["L0_ffn_sdown_w"].T),
        "ident": np.eye(128, dtype=np.float32),
        "ones_row": np.ones((1, 128), dtype=np.float32),
    }
    W_dt = inputs["L0_attn_W_dt"]
    b_dt = inputs["L0_attn_b_dt"]
    A_log = inputs["L0_attn_A_log"]
    D_par = inputs["L0_attn_D_param"]
    in_maps = []
    for c in range(NCORE):
        b, q = c // 4, c % 4
        g = q
        m = dict(common)
        m["x_my"] = np.ascontiguousarray(
            x[b].reshape(-1, 4, cfg.TF, DM)[:, q].reshape(TPC, DM))
        m["W_dt_sl"] = np.ascontiguousarray(W_dt[:, g * cfg.DG:(g + 1) * cfg.DG])
        m["b_dt_sl"] = np.ascontiguousarray(
            b_dt[g * cfg.DG:(g + 1) * cfg.DG].reshape(cfg.NDT, 128).T)
        m["A_log_sl"] = np.ascontiguousarray(A_log[g * cfg.DG:(g + 1) * cfg.DG, :])
        m["D_sl"] = np.ascontiguousarray(
            D_par[g * cfg.DG:(g + 1) * cfg.DG].reshape(cfg.NDT, 128).T)
        m["cinfo"] = np.array(
            [[g * cfg.NDT * cfg.TS, q * cfg.TF, 0, 0]], dtype=np.uint32)
        in_maps.append(m)
    return in_maps


def kernel(**inputs):
    cfg = CFG()
    x = inputs["L0_attn_x"]
    nc = build_program(cfg)
    in_maps = make_in_maps(cfg, inputs)
    res = run_bass_kernel_spmd(nc, in_maps, core_ids=list(range(NCORE)))
    out = np.empty_like(np.ascontiguousarray(x))
    TPC = cfg.TPC
    for c in range(NCORE):
        b, q = c // 4, c % 4
        out[b].reshape(-1, 4, cfg.TF, DM)[:, q] = \
            res.results[c]["out_my"].reshape(-1, cfg.TF, DM)
    return out


# version 08
# speedup vs baseline: 1.0734x; 1.0734x over previous
import numpy as np
import ml_dtypes
import concourse.bass as bass
import concourse.mybir as mybir
from concourse.bass_utils import run_bass_kernel_spmd

F32 = mybir.dt.float32
BF16 = mybir.dt.bfloat16
U32 = mybir.dt.uint32
AF = mybir.ActivationFunctionType
ALU = mybir.AluOpType
AX = mybir.AxisListType

DM = 1024
DI = 2048
DST = 16
NE = 32
HE = 128
HSH = 512
NCORE = 8
BIGNEG = 1.0e30
LN_EPS = 1.0e-5
NKT_DM = DM // 128
NKT_DI = DI // 128
PIPE_AGY = True
PIPE_AGU = True


class CFG:
    def __init__(self, T=16384, TF=512, TS=512):
        self.T = T
        self.TPC = T // 4
        self.TF = TF
        self.TS = TS
        self.NCHF = self.TPC // TF
        self.NCHS = T // TS
        self.SBK = 4
        self.NSB = DST // self.SBK
        self.DG = DI // 4
        self.NDT = self.DG // 128


class Walk:
    ENGS = ("sy", "gp", "ve", "se", "te")

    def __init__(self, nc, E, sems):
        self.nc = nc
        self.E = E
        self.sems = sems
        self.c = {k: 0 for k in sems}
        self.eng = {"sy": nc.sync, "gp": nc.gpsimd, "ve": nc.vector,
                    "se": nc.scalar, "te": nc.tensor}[E]

    def snap(self, name):
        return self.c[name]

    def _bump(self, name, instr, k):
        self.c[name] += k
        if instr is not None:
            instr.then_inc(self.sems[name], k)
        return instr

    def wait(self, on_eng, name, val=None):
        v = self.c[name] if val is None else val
        if v <= 0:
            return
        if self.E == on_eng:
            self.eng.wait_ge(self.sems[name], v)

    def ve_(self, fn):
        return self._bump("VE", fn(self.nc.vector) if self.E == "ve" else None, 1)

    def se_(self, fn):
        return self._bump("SE", fn(self.nc.scalar) if self.E == "se" else None, 1)

    def te_(self, fn):
        return self._bump("TE", fn(self.nc.tensor) if self.E == "te" else None, 1)

    def gp_(self, fn):
        return self._bump("GP", fn(self.nc.gpsimd) if self.E == "gp" else None, 1)

    def dma_sy(self, out, in_, stream="LD"):
        i = self.nc.sync.dma_start(out=out, in_=in_) if self.E == "sy" else None
        return self._bump(stream, i, 16)

    def dma_se(self, out, in_):
        i = self.nc.scalar.dma_start(out=out, in_=in_) if self.E == "se" else None
        return self._bump("ST", i, 16)

    def dma_gp(self, out, in_, stream="LD"):
        i = self.nc.gpsimd.dma_start(out=out, in_=in_) if self.E == "gp" else None
        return self._bump(stream, i, 16)

    def dma_gp_dyn(self, fn):
        i = fn(self.nc.gpsimd) if self.E == "gp" else None
        return self._bump("GD", i, 16)

    def ag(self, ins, outs, groups):
        i = None
        if self.E == "gp":
            i = self.nc.gpsimd.collective_compute(
                "AllGather", ALU.bypass, replica_groups=groups, ins=ins, outs=outs)
        return self._bump("AG", i, 1)


def ceil32(x):
    return (x + 31) & ~31


class Arena:
    def __init__(self, t, cols, off=0):
        self.t = t
        self.cols = cols
        self.off = off

    def take(self, cols):
        o = ceil32(self.off)
        assert o + cols <= self.cols, \
            f"{self.t.name} overflow: {o}+{cols}>{self.cols}"
        self.off = o + cols
        return self.t[:, o:o + cols]


def build_program(cfg: CFG):
    nc = bass.Bass("TRN2", target_bir_lowering=False, debug=False)
    T, TF, TS, TPC = cfg.T, cfg.TF, cfg.TS, cfg.TPC
    NCHF, NCHS, SBK, NSB, DG, NDT = (cfg.NCHF, cfg.NCHS, cfg.SBK, cfg.NSB,
                                     cfg.DG, cfg.NDT)
    NTT = TF // 128

    def ein(name, shape, dt=F32):
        return nc.dram_tensor(name, shape, dt, kind="ExternalInput").ap()

    x_my = ein("x_my", [TPC, DM])
    W_in = ein("W_in", [DM, 2 * DI])
    b_in_u = ein("b_in_u", [128, DI // 128])
    b_in_z = ein("b_in_z", [128, DI // 128])
    W_x = ein("W_x", [DI, 2 * DST])
    b_x = ein("b_x", [2 * DST, 1])
    W_dt_sl = ein("W_dt_sl", [DI, DG])
    b_dt_sl = ein("b_dt_sl", [128, NDT])
    W_out = ein("W_out", [DI, DM])
    b_out_c = ein("b_out_c", [128, DM // 128])
    A_log_sl = ein("A_log_sl", [DG, DST])
    D_sl = ein("D_sl", [128, NDT])
    r_wT = ein("r_wT", [128, NKT_DM * NE])
    r_b = ein("r_b", [NE, 1])
    gate_wT = ein("gate_wT", [128, NE * NKT_DM * 128])
    up_wT = ein("up_wT", [128, NE * NKT_DM * 128])
    down_eT = ein("down_eT", [128, NE * DM])
    sg_wT = ein("sg_wT", [128, NKT_DM * HSH])
    su_wT = ein("su_wT", [128, NKT_DM * HSH])
    sd_wT = ein("sd_wT", [128, (HSH // 128) * DM])
    ident = ein("ident", [128, 128])
    ones_row = ein("ones_row", [1, 128])
    cinfo = ein("cinfo", [1, 4], U32)

    out_my = nc.dram_tensor("out_my", [TPC, DM], F32, kind="ExternalOutput").ap()

    def idram(name, shape, dt):
        return nc.dram_tensor(name, shape, dt, kind="Internal").ap()

    x1_d = idram("x1_d", [DM, TPC], F32)
    z_d = idram("z_d", [DI, TPC], BF16)
    u_in2 = idram("u_in2", [4, DI, TPC // 4], BF16)
    u_ag2 = idram("u_ag2", [4, 4 * DI, TPC // 4], BF16)
    y_in2 = idram("y_in2", [4, DG, T // 4], BF16)
    y_ag2 = idram("y_ag2", [4, DI, T // 4], BF16)
    bcT_d = idram("bcT_d", [2 * DST, T], BF16)
    wT_d = idram("wT_d", [NE, TPC], BF16)
    y_gs_d = idram("y_gs_d", [DI, TPC], BF16)
    win_bf = idram("win_bf", [128, DM * 2 * DI // 128], BF16)
    gate_bf = idram("gate_bf", [128, NE * NKT_DM * 128], BF16)
    up_bf = idram("up_bf", [128, NE * NKT_DM * 128], BF16)
    down_bf = idram("down_bf", [128, NE * DM], BF16)
    sg_bf = idram("sg_bf", [128, NKT_DM * HSH], BF16)
    su_bf = idram("su_bf", [128, NKT_DM * HSH], BF16)
    sd_bf = idram("sd_bf", [128, (HSH // 128) * DM], BF16)
    wout_bf = idram("wout_bf", [DI, DM], BF16)
    rw_bf = idram("rw_bf", [128, NKT_DM * NE], BF16)

    identt = nc.alloc_sbuf_tensor("identt", [128, 128], F32)
    onesr = nc.alloc_sbuf_tensor("onesr", [1, 128], F32)
    onesc_bf = nc.alloc_sbuf_tensor("onesc_bf", [128, 1], BF16)
    cinfot = nc.alloc_sbuf_tensor("cinfot", [1, 4], U32)
    A_cols = nc.alloc_sbuf_tensor("A_cols", [128, NDT * DST], F32)
    D_col = nc.alloc_sbuf_tensor("D_col", [128, NDT], F32)
    bdt_col = nc.alloc_sbuf_tensor("bdt_col", [128, NDT], F32)
    bxc = nc.alloc_sbuf_tensor("bxc", [2 * DST, 1], F32)
    rbc = nc.alloc_sbuf_tensor("rbc", [NE, 1], F32)
    binu_c = nc.alloc_sbuf_tensor("binu_c", [128, DI // 128], F32)
    binz_c = nc.alloc_sbuf_tensor("binz_c", [128, DI // 128], F32)
    bout_c = nc.alloc_sbuf_tensor("bout_c", [128, DM // 128], F32)
    hcar = nc.alloc_sbuf_tensor("hcar", [128, NDT * DST], F32)
    wx_bf = nc.alloc_sbuf_tensor("wx_bf", [128, NKT_DI * 2 * DST], BF16)
    rwt = nc.alloc_sbuf_tensor("rwt", [128, NKT_DM * NE], BF16)
    cst = nc.alloc_sbuf_tensor("cst", [128, 2], F32)

    F32_COLS = 20000
    BF_COLS = 64500
    af_t = nc.alloc_sbuf_tensor("af", [128, F32_COLS], F32)
    ab_t = nc.alloc_sbuf_tensor("ab", [128, BF_COLS], BF16)

    pA = nc.alloc_psum_tensor("pA", [128, 512], F32)
    pB = nc.alloc_psum_tensor("pB", [128, 512], F32)
    pC = nc.alloc_psum_tensor("pC", [128, 512], F32)
    pD = nc.alloc_psum_tensor("pD", [128, 512], F32)
    pE = nc.alloc_psum_tensor("pE", [128, 512], F32)
    pF = nc.alloc_psum_tensor("pF", [128, 512], F32)
    pG = nc.alloc_psum_tensor("pG", [128, 512], F32)
    pH = nc.alloc_psum_tensor("pH", [128, 512], F32)
    pALL = [pA, pB, pC, pD, pE, pF, pG, pH]

    sems = {}
    for s in ["LD", "LDR", "LDX", "U0", "U1", "B0", "B1", "W0", "W1",
              "I0", "I1", "S0", "S1", "O0", "O1", "DN", "RW",
              "ST", "GD", "VE", "SE", "TE", "GP", "AG"]:
        sems[s] = nc.semaphore(s).__enter__()

    af = Arena(af_t, F32_COLS)
    xtm = af.take(NTT * DM)
    x_res = af.take(NKT_DM * TF)
    x1c = af.take(NKT_DM * TF)
    mrow = af.take(TF)
    vrow = af.take(TF)
    rstdr = af.take(TF)
    m2row = af.take(TF)
    mrep = af.take(TF)
    rrep = af.take(TF)
    lgsb = af.take(TF)
    lgT = af.take(NTT * NE)
    wrk = af.take(NE)
    mks = af.take(4)
    nm1 = af.take(32)
    e4 = af.take(4)
    s4 = af.take(32)
    r4 = af.take(32)
    wv4 = af.take(4)
    w32 = af.take(NE)
    masks = af.take(4 * NE)
    wTsb = af.take(TF)
    h1t = af.take(2 * TF)
    tmp8 = xtm[:, 0:NKT_DM * TF]

    afp = Arena(af_t, F32_COLS)
    cast_f = afp.take(8192)

    afs = Arena(af_t, F32_COLS)
    ug_f = [afs.take(NDT * TS) for _ in range(2)]
    dt_f = [afs.take(NDT * TS) for _ in range(2)]
    yacc = afs.take(NDT * TS)
    yred = afs.take(TS)
    spA = [[afs.take(TS) for _ in range(2)] for _ in range(2)]

    ab = Arena(ab_t, BF_COLS)
    xbf = ab.take(NKT_DM * TF)
    xnbf = ab.take(NKT_DM * TF)
    hcache = ab.take(NE * TF)
    hsc = ab.take((HSH // 128) * TF)
    xsqbf = hcache[:, 0:NKT_DM * TF]
    wg4 = [ab.take(4 * NKT_DM * 128) for _ in range(2)]
    wu4 = [ab.take(4 * NKT_DM * 128) for _ in range(2)]
    wdn = [ab.take(NE * 128)]
    wshg = [ab.take(NKT_DM * 128) for _ in range(2)]
    wshu = [ab.take(NKT_DM * 128) for _ in range(2)]
    wshd = [ab.take((HSH // 128) * 128) for _ in range(2)]
    wint = [ab.take(NKT_DM * 128) for _ in range(2)]
    sgbf = ab.take(2 * TF)
    wrep = ab.take(4 * TF)
    wTsb_bf = ab.take(TF)
    ubf_sb = ab.take(TF)
    zbf_sb = ab.take(TF)
    AB_END = ab.off

    abp = Arena(ab_t, BF_COLS)
    cast_b = abp.take(8192)

    abs_ = Arena(ab_t, BF_COLS)
    u_full = [abs_.take(NKT_DI * TS) for _ in range(2)]
    ybf_s = abs_.take(NDT * TS)
    wdt_bf = abs_.take(NKT_DI * NDT * 128)
    dtu_b = [abs_.take(NDT * TS) for _ in range(2)]
    bc_sb = [abs_.take(TS) for _ in range(2)]
    slab0 = abs_.off
    dA_bufs = [abs_.take(SBK * TS) for _ in range(2)]
    w_sl = abs_.take(SBK * TS)
    h_sl = abs_.take(SBK * TS)
    brep = [abs_.take(SBK * TS) for _ in range(2)]
    crep = [abs_.take(SBK * TS) for _ in range(2)]
    _gb_off = ceil32(abs_.off)
    assert _gb_off + NKT_DI * TF <= BF_COLS, "gbulk overflow"
    gbulk = ab_t[:, _gb_off:_gb_off + NKT_DI * TF]

    abc = Arena(ab_t, BF_COLS, off=AB_END)
    ymk = abc.take(TF)
    zch = abc.take(TF)
    szt = abc.take(TF)
    wotk = [abc.take(NKT_DM * 128) for _ in range(2)]

    def walk(w: Walk):
        regs = {}
        if w.E == "gp":
            regs["y"] = nc.gpsimd.alloc_register("r_yoff")
            regs["t"] = nc.gpsimd.alloc_register("r_tmp")

        w.dma_sy(identt[:, :], ident)
        w.dma_sy(onesr[:, :], ones_row)
        w.dma_sy(cinfot[:, :], cinfo)
        w.dma_sy(bxc[:, :], b_x)
        w.dma_sy(rbc[:, :], r_b)
        w.dma_sy(binu_c[:, :], b_in_u)
        w.dma_sy(binz_c[:, :], b_in_z)
        w.dma_sy(bout_c[:, :], b_out_c)
        w.dma_sy(bdt_col[:, :], b_dt_sl)
        w.dma_sy(D_col[:, :], D_sl)
        for dtl in range(NDT):
            w.dma_sy(A_cols[:, dtl * DST:(dtl + 1) * DST],
                     A_log_sl[dtl * 128:(dtl + 1) * 128, :])
        ld_pre = w.snap("LD")
        w.wait("gp", "LD")
        if w.E == "gp":
            nc.gpsimd.reg_load(regs["y"], cinfot[0:1, 1:2])
        w.gp_(lambda g: g.nop())
        w.wait("se", "LD")
        if w.E == "se":
            regs["su"] = nc.scalar.alloc_register("r_su")
            nc.scalar.reg_load(regs["su"], cinfot[0:1, 0:1])
            regs["sU0"] = nc.scalar.alloc_register("r_sU0")
            nc.scalar.reg_alu(regs["sU0"], regs["su"], u_full[0].offset, ALU.add)
        w.se_(lambda s: s.nop())
        w.wait("ve", "LD")
        if w.E == "ve":
            regs["vu"] = nc.vector.alloc_register("r_vu")
            nc.vector.reg_load(regs["vu"], cinfot[0:1, 0:1])
            regs["vU0"] = nc.vector.alloc_register("r_vU0")
            nc.vector.reg_alu(regs["vU0"], regs["vu"], u_full[0].offset, ALU.add)
        w.ve_(lambda v: v.nop())


        w.wait("ve", "LD")
        w.ve_(lambda v: v.memset(onesc_bf[:, :], 1.0))
        w.ve_(lambda v: v.memset(cst[:, 0:1], LN_EPS))
        w.ve_(lambda v: v.memset(cst[:, 1:2], 1.0))
        w.ve_(lambda v: v.memset(hcar[:, :], 0.0))
        w.wait("se", "LD")
        w.se_(lambda s: s.activation(out=A_cols[:, :], in_=A_cols[:, :], func=AF.Exp))
        w.wait("ve", "SE")
        w.ve_(lambda v: v.tensor_scalar_mul(A_cols[:, :], A_cols[:, :], -1.0))

        def cast_cache(src_ap, dst_ap, numel):
            cols = numel // 128
            CC = 8192
            pos = 0
            sf = src_ap.rearrange("a b -> (a b)").rearrange("(p c) -> p c", p=128)
            df = dst_ap.rearrange("a b -> (a b)").rearrange("(p c) -> p c", p=128)
            while pos < cols:
                n = min(CC, cols - pos)
                w.wait("sy", "SE")
                w.dma_sy(cast_f[:, 0:n], sf[:, pos:pos + n])
                w.wait("se", "LD")
                w.wait("se", "ST")
                w.se_(lambda s, n=n: s.copy(cast_b[:, 0:n], cast_f[:, 0:n]))
                w.wait("se", "SE")
                w.dma_se(df[:, pos:pos + n], cast_b[:, 0:n])
                pos += n

        cast_cache(W_in, win_bf, DM * 2 * DI)
        cast_cache(gate_wT, gate_bf, NE * DM * HE)
        cast_cache(up_wT, up_bf, NE * DM * HE)
        cast_cache(down_eT, down_bf, NE * HE * DM)
        cast_cache(sg_wT, sg_bf, DM * HSH)
        cast_cache(su_wT, su_bf, DM * HSH)
        cast_cache(sd_wT, sd_bf, HSH * DM)
        cast_cache(W_out, wout_bf, DI * DM)
        cast_cache(r_wT, rw_bf, DM * NE)
        w.wait("sy", "ST")
        w.dma_sy(rwt[:, :], rw_bf[:, :], stream="RW")
        for kt in range(NKT_DI):
            w.wait("sy", "SE")
            w.dma_sy(cast_f[:, 0:2 * DST], W_x[kt * 128:(kt + 1) * 128, :])
            w.wait("se", "LD")
            w.se_(lambda s, kt=kt: s.copy(
                wx_bf[:, kt * 2 * DST:(kt + 1) * 2 * DST], cast_f[:, 0:2 * DST]))
        st_pre = w.snap("ST")
        se_pre = w.snap("SE")
        w.wait("gp", "ST", st_pre)
        w.wait("gp", "SE", se_pre)

        glob_hist = {"sh": [0, 0]}

        def ln_norm(src_f32, src_bf):
            w.wait("se", "SE")
            w.wait("se", "TE")
            w.se_(lambda s: s.activation(out=xsqbf[:, :], in_=src_bf[:, :],
                                         func=AF.Square))
            se_sq = w.snap("SE")
            w.wait("te", "SE", se_sq)
            w.wait("te", "VE")
            for kt in range(NKT_DM):
                w.te_(lambda t, kt=kt: t.matmul(
                    pE[0:1, 0:TF], onesc_bf[:, :], src_bf[:, kt * TF:(kt + 1) * TF],
                    start=(kt == 0), stop=(kt == NKT_DM - 1)))
            for kt in range(NKT_DM):
                w.te_(lambda t, kt=kt: t.matmul(
                    pE[32:33, 0:TF], onesc_bf[:, :], xsqbf[:, kt * TF:(kt + 1) * TF],
                    start=(kt == 0), stop=(kt == NKT_DM - 1)))
            te_stats = w.snap("TE")
            w.wait("ve", "TE", te_stats)
            w.ve_(lambda v: v.tensor_scalar_mul(mrow[0:1, :], pE[0:1, 0:TF], 1.0 / DM))
            w.wait("ve", "VE")
            w.ve_(lambda v: v.tensor_scalar_mul(vrow[0:1, :], pE[32:33, 0:TF], 1.0 / DM))
            w.wait("ve", "VE")
            w.ve_(lambda v: v.tensor_mul(m2row[0:1, :], mrow[0:1, :], mrow[0:1, :]))
            w.wait("ve", "VE")
            w.ve_(lambda v: v.tensor_sub(vrow[0:1, :], vrow[0:1, :], m2row[0:1, :]))
            ve_var = w.snap("VE")
            w.wait("se", "VE", ve_var)
            w.se_(lambda s: s.activation(out=rstdr[0:1, :], in_=vrow[0:1, :],
                                         func=AF.Ln, bias=cst[0:1, 0:1]))
            w.wait("se", "SE")
            w.se_(lambda s: s.activation(out=rstdr[0:1, :], in_=rstdr[0:1, :],
                                         func=AF.Exp, scale=-0.5))
            se_rstd = w.snap("SE")
            w.wait("te", "SE", se_rstd)
            w.te_(lambda t: t.matmul(pF[:, 0:TF], onesr[:, :], mrow[0:1, :],
                                     start=True, stop=True))
            w.te_(lambda t: t.matmul(pE[:, 0:TF], onesr[:, :], rstdr[0:1, :],
                                     start=True, stop=True))
            te_bc = w.snap("TE")
            w.wait("se", "TE", te_bc)
            w.se_(lambda s: s.copy(mrep[:, :], pF[:, 0:TF]))
            w.wait("se", "SE")
            w.se_(lambda s: s.copy(rrep[:, :], pE[:, 0:TF]))
            se_bc = w.snap("SE")
            w.wait("ve", "SE", se_bc)
            w.wait("ve", "TE")
            w.wait("ve", "ST")
            for kt in range(NKT_DM):
                w.ve_(lambda v, kt=kt: v.tensor_sub(
                    tmp8[:, kt * TF:(kt + 1) * TF],
                    src_f32[:, kt * TF:(kt + 1) * TF], mrep[:, :]))
            w.wait("ve", "VE")
            for kt in range(NKT_DM):
                w.ve_(lambda v, kt=kt: v.tensor_mul(
                    xnbf[:, kt * TF:(kt + 1) * TF],
                    tmp8[:, kt * TF:(kt + 1) * TF], rrep[:, :]))
            return w.snap("VE")

        def ffn(resid_src, resid_dst, ci):
            w.wait("te", "RW", 16)
            w.wait("te", "SE")
            w.wait("te", "VE")
            for kt in range(NKT_DM):
                w.te_(lambda t, kt=kt: t.matmul(
                    pE[0:NE, 0:TF], rwt[:, kt * NE:(kt + 1) * NE],
                    xnbf[:, kt * TF:(kt + 1) * TF],
                    start=(kt == 0), stop=(kt == NKT_DM - 1)))
            te_rt = w.snap("TE")
            w.wait("se", "TE", te_rt)
            w.se_(lambda s: s.activation(out=lgsb[0:NE, 0:TF], in_=pE[0:NE, 0:TF],
                                         func=AF.Identity, bias=rbc[:, :]))
            for tb in range(NTT):
                pt = pE if tb % 2 == 0 else pF
                w.wait("te", "SE")
                w.te_(lambda t, tb=tb, pt=pt: t.transpose(
                    pt[:, 0:NE], lgsb[0:NE, tb * 128:(tb + 1) * 128],
                    identt[0:NE, 0:NE]))
                w.wait("se", "TE")
                w.se_(lambda s, tb=tb, pt=pt: s.copy(
                    lgT[:, tb * NE:(tb + 1) * NE], pt[:, 0:NE]))
            se_lgT = w.snap("SE")

            for tb in range(NTT):
                blk = lgT[:, tb * NE:(tb + 1) * NE]
                w.wait("ve", "SE", se_lgT)
                w.wait("ve", "TE")
                w.ve_(lambda v: v.tensor_copy(wrk[:, :], blk))
                for k in range(4):
                    w.wait("ve", "VE")
                    w.ve_(lambda v, k=k: v.reduce_max(mks[:, k:k + 1], wrk[:, :], AX.X))
                    w.wait("ve", "VE")
                    w.ve_(lambda v, k=k: v.tensor_scalar(
                        masks[:, k * NE:(k + 1) * NE], wrk[:, :], mks[:, k:k + 1],
                        None, ALU.is_equal))
                    if k < 3:
                        w.wait("ve", "VE")
                        w.ve_(lambda v, k=k: v.scalar_tensor_tensor(
                            wrk[:, :], masks[:, k * NE:(k + 1) * NE], -BIGNEG,
                            wrk[:, :], ALU.mult, ALU.add))
                w.wait("ve", "VE")
                w.ve_(lambda v: v.tensor_scalar_mul(nm1[:, 0:1], mks[:, 0:1], -1.0))
                ve_nm1 = w.snap("VE")
                w.wait("se", "VE", ve_nm1)
                w.se_(lambda s: s.activation(out=e4[:, :], in_=mks[:, :],
                                             func=AF.Exp, bias=nm1[:, 0:1]))
                se_e4 = w.snap("SE")
                w.wait("ve", "SE", se_e4)
                w.ve_(lambda v: v.reduce_sum(s4[:, 0:1], e4[:, :], AX.X))
                w.wait("ve", "VE")
                w.ve_(lambda v: v.reciprocal(r4[:, 0:1], s4[:, 0:1]))
                w.wait("ve", "VE")
                w.ve_(lambda v: v.tensor_scalar_mul(wv4[:, :], e4[:, :], r4[:, 0:1]))
                w.wait("ve", "VE")
                w.ve_(lambda v: v.tensor_scalar(
                    w32[:, :], masks[:, 0:NE], wv4[:, 0:1], None, ALU.mult))
                for k in range(1, 4):
                    w.wait("ve", "VE")
                    w.ve_(lambda v, k=k: v.scalar_tensor_tensor(
                        w32[:, :], masks[:, k * NE:(k + 1) * NE], wv4[:, k:k + 1],
                        w32[:, :], ALU.mult, ALU.add))
                ve_w32 = w.snap("VE")
                pt = pE if tb % 2 == 0 else pF
                w.wait("te", "VE", ve_w32)
                w.wait("te", "SE")
                w.te_(lambda t, pt=pt: t.transpose(pt[0:NE, 0:128], w32[:, :],
                                                   identt[:, :]))
                w.wait("se", "TE")
                w.se_(lambda s, tb=tb, pt=pt: s.copy(
                    wTsb[0:NE, tb * 128:(tb + 1) * 128], pt[0:NE, 0:128]))
            se_wT = w.snap("SE")
            w.wait("se", "SE", se_wT)
            w.wait("se", "ST")
            w.se_(lambda s: s.copy(wTsb_bf[0:NE, 0:TF], wTsb[0:NE, 0:TF]))
            w.wait("se", "SE")
            w.dma_se(wT_d[:, ci * TF:(ci + 1) * TF], wTsb_bf[0:NE, 0:TF])
            st_wT = w.snap("ST")

            def load_wrep(g4):
                w.wait("sy", "ST", st_wT)
                w.wait("sy", "VE")
                src = wT_d[g4 * 4:(g4 + 1) * 4, ci * TF:(ci + 1) * TF]
                src = src.rearrange("e (o c) -> e o c", o=1).broadcast_to(
                    [4, 128, TF]).transpose([1, 0, 2])
                w.dma_sy(wrep[:, :].rearrange("p (e c) -> p e c", e=4), src,
                         stream="LDR")
                return w.snap("LDR")

            def load_wg4(g4, te_free):
                gb = g4 % 2
                GC = 4 * NKT_DM * 128
                w.wait("sy", "TE", te_free)
                w.dma_sy(wg4[gb][:, :], gate_bf[:, g4 * GC:(g4 + 1) * GC],
                         stream=f"W{gb}")
                w.wait("gp", "TE", te_free)
                w.dma_gp(wu4[gb][:, :], up_bf[:, g4 * GC:(g4 + 1) * GC],
                         stream=f"W{gb}")
                return w.snap(f"W{gb}"), gb

            ld_wrep = load_wrep(0)
            ld_wg, gbuf = load_wg4(0, 0)
            gbuf_of = {0: 0}
            te_grp_hist = {}
            sg_hist = [0, 0]
            h1_hist = [0, 0]
            h2_hist = [0, 0]
            for e in range(NE):
                if e % 4 == 0 and e > 0:
                    ld_wrep = load_wrep(e // 4)
                if e % 4 == 0 and e > 0:
                    g4 = e // 4
                    te_free = te_grp_hist.get(g4 - 2, 0)
                    ld_wg, gbuf = load_wg4(g4, te_free)
                    gbuf_of[g4] = gbuf
                eo = (e % 4) * NKT_DM * 128
                pg = pA if e % 2 == 0 else pB
                pu = pC if e % 2 == 0 else pD
                sgb = sgbf[:, (e % 2) * TF:((e % 2) + 1) * TF]
                w.wait("te", "SE", sg_hist[0])
                w.wait("te", "VE", h1_hist[0])
                gb = gbuf_of[e // 4]
                w.wait("te", f"W{gb}")
                for kt in range(NKT_DM):
                    w.te_(lambda t, kt=kt, eo=eo, pg=pg, gb=gb: t.matmul(
                        pg[:, 0:TF], wg4[gb][:, eo + kt * 128: eo + (kt + 1) * 128],
                        xnbf[:, kt * TF:(kt + 1) * TF],
                        start=(kt == 0), stop=(kt == NKT_DM - 1)))
                te_g = w.snap("TE")
                for kt in range(NKT_DM):
                    w.te_(lambda t, kt=kt, eo=eo, pu=pu, gb=gb: t.matmul(
                        pu[:, 0:TF], wu4[gb][:, eo + kt * 128: eo + (kt + 1) * 128],
                        xnbf[:, kt * TF:(kt + 1) * TF],
                        start=(kt == 0), stop=(kt == NKT_DM - 1)))
                te_u = w.snap("TE")
                if e % 4 == 3:
                    te_grp_hist[e // 4] = te_u
                w.wait("se", "TE", te_g)
                w.wait("se", "VE", h1_hist[1])
                w.se_(lambda s, pg=pg, sgb=sgb: s.activation(
                    out=sgb, in_=pg[:, 0:TF], func=AF.Silu))
                se_sg = w.snap("SE")
                w.wait("ve", "TE", te_u)
                w.wait("ve", "SE", se_sg)
                w.wait("ve", "LDR")
                h1b = h1t[:, (e % 2) * TF:((e % 2) + 1) * TF]
                w.ve_(lambda v, pu=pu, sgb=sgb, h1b=h1b: v.tensor_mul(
                    h1b, sgb, pu[:, 0:TF]))
                ve_h1 = w.snap("VE")
                w.wait("ve", "VE", ve_h1)
                w.ve_(lambda v, e=e, h1b=h1b: v.tensor_mul(
                    hcache[:, e * TF:(e + 1) * TF], h1b,
                    wrep[:, (e % 4) * TF:((e % 4) + 1) * TF]))
                sg_hist = [sg_hist[1], se_sg]
                h1_hist = [h1_hist[1], ve_h1]
                h2_hist = [h2_hist[1], w.snap("VE")]

            sh_sg_hist = [w.snap("SE"), w.snap("SE")]
            sh_h_hist = [h2_hist[0], h2_hist[1]]
            for ms in range(HSH // 128):
                buf = ms % 2
                pg = pA if buf == 0 else pB
                pu = pC if buf == 0 else pD
                sgb = sgbf[:, buf * TF:(buf + 1) * TF]
                w.wait("sy", "TE", glob_hist["sh"][buf])
                w.dma_sy(wshg[buf][:, :],
                         sg_bf[:, ms * NKT_DM * 128:(ms + 1) * NKT_DM * 128],
                         stream=f"S{buf}")
                w.wait("gp", "TE", glob_hist["sh"][buf])
                w.dma_gp(wshu[buf][:, :],
                         su_bf[:, ms * NKT_DM * 128:(ms + 1) * NKT_DM * 128],
                         stream=f"S{buf}")
                w.wait("te", f"S{buf}")
                w.wait("te", "SE", sh_sg_hist[0] if ms >= 2 else w.snap("SE"))
                w.wait("te", "VE", sh_h_hist[0] if ms >= 2 else w.snap("VE"))
                for kt in range(NKT_DM):
                    w.te_(lambda t, kt=kt, buf=buf, pg=pg: t.matmul(
                        pg[:, 0:TF], wshg[buf][:, kt * 128:(kt + 1) * 128],
                        xnbf[:, kt * TF:(kt + 1) * TF],
                        start=(kt == 0), stop=(kt == NKT_DM - 1)))
                te_g = w.snap("TE")
                for kt in range(NKT_DM):
                    w.te_(lambda t, kt=kt, buf=buf, pu=pu: t.matmul(
                        pu[:, 0:TF], wshu[buf][:, kt * 128:(kt + 1) * 128],
                        xnbf[:, kt * TF:(kt + 1) * TF],
                        start=(kt == 0), stop=(kt == NKT_DM - 1)))
                te_u = w.snap("TE")
                w.wait("se", "TE", te_g)
                w.wait("se", "VE", sh_h_hist[1] if ms >= 1 else w.snap("VE"))
                w.se_(lambda s, pg=pg, sgb=sgb: s.activation(
                    out=sgb, in_=pg[:, 0:TF], func=AF.Silu))
                se_sg = w.snap("SE")
                w.wait("ve", "TE", te_u)
                w.wait("ve", "SE", se_sg)
                w.ve_(lambda v, ms=ms, pu=pu, sgb=sgb: v.tensor_mul(
                    hsc[:, ms * TF:(ms + 1) * TF], sgb, pu[:, 0:TF]))
                sh_sg_hist = [sh_sg_hist[1], se_sg]
                sh_h_hist = [sh_h_hist[1], w.snap("VE")]
                glob_hist["sh"][buf] = te_u

            ve_all_h = w.snap("VE")
            dn_hist = [0, 0]
            dn_te_hist = [0, 0]
            for m in range(DM // 128):
                buf = 0
                pd = pG if m % 2 == 0 else pH
                if m % 2 == 0:
                    w.wait("sy", "TE", dn_te_hist[1] if m >= 1 else 0)
                    w.dma_sy(wdn[buf][:, :],
                             down_bf[:, m * NE * 128:(m + 1) * NE * 128],
                             stream="DN")
                else:
                    w.wait("gp", "TE", dn_te_hist[1] if m >= 1 else 0)
                    w.dma_gp(wdn[buf][:, :],
                             down_bf[:, m * NE * 128:(m + 1) * NE * 128],
                             stream="DN")
                sdb = m % 2
                w.wait("gp", "TE", dn_te_hist[1] if m >= 1 else 0)
                w.dma_gp(wshd[sdb][:, :],
                         sd_bf[:, m * (HSH // 128) * 128:
                               (m + 1) * (HSH // 128) * 128], stream=f"S{sdb}")
                w.wait("te", "DN")
                w.wait("te", f"S{sdb}")
                w.wait("te", "VE", max(ve_all_h, dn_hist[0]))
                for e in range(NE):
                    w.te_(lambda t, e=e, buf=buf, pd=pd: t.matmul(
                        pd[:, 0:TF], wdn[buf][:, e * 128:(e + 1) * 128],
                        hcache[:, e * TF:(e + 1) * TF],
                        start=(e == 0), stop=False))
                for ks in range(HSH // 128):
                    w.te_(lambda t, ks=ks, pd=pd, buf=m % 2: t.matmul(
                        pd[:, 0:TF], wshd[buf][:, ks * 128:(ks + 1) * 128],
                        hsc[:, ks * TF:(ks + 1) * TF],
                        start=False, stop=(ks == HSH // 128 - 1)))
                te_dn = w.snap("TE")
                dn_te_hist = [dn_te_hist[1], te_dn]
                w.wait("ve", "TE", te_dn)
                w.ve_(lambda v, m=m, pd=pd: v.scalar_tensor_tensor(
                    resid_dst[:, m * TF:(m + 1) * TF], pd[:, 0:TF], 0.5,
                    resid_src[:, m * TF:(m + 1) * TF], ALU.mult, ALU.add))
                dn_hist = [dn_hist[1], w.snap("VE")]
            return w.snap("VE")

        w.wait("sy", "SE", se_pre)
        w.wait("sy", "ST", st_pre)
        for ci in range(NCHF):
            w.wait("sy", "VE")
            w.wait("sy", "SE")
            w.wait("sy", "TE")
            for tt in range(NTT):
                w.dma_sy(xtm[:, tt * DM:(tt + 1) * DM],
                         x_my[ci * TF + tt * 128: ci * TF + (tt + 1) * 128, :],
                         stream="LDX")
            for dk in range(NKT_DM):
                pt = pE if dk % 2 == 0 else pF
                w.wait("te", "LDX")
                w.wait("te", "SE")
                for tt in range(NTT):
                    w.te_(lambda t, dk=dk, tt=tt, pt=pt: t.transpose(
                        pt[:, tt * 128:(tt + 1) * 128],
                        xtm[:, tt * DM + dk * 128: tt * DM + (dk + 1) * 128],
                        identt[:, :]))
                te_tr = w.snap("TE")
                w.wait("se", "TE", te_tr)
                w.se_(lambda s, dk=dk, pt=pt: s.copy(
                    x_res[:, dk * TF:(dk + 1) * TF], pt[:, 0:TF]))
            w.wait("se", "SE")
            w.se_(lambda s: s.copy(xbf[:, :], x_res[:, :]))

            ln_norm(x_res, xbf)
            ffn(x_res, x1c, ci)
            ve_x1 = w.snap("VE")

            w.wait("se", "VE", ve_x1)
            for dk in range(NKT_DM):
                w.dma_se(x1_d[dk * 128:(dk + 1) * 128, ci * TF:(ci + 1) * TF],
                         x1c[:, dk * TF:(dk + 1) * TF])
            w.se_(lambda s: s.copy(xbf[:, :], x1c[:, :]))
            ln_norm(x1c, xbf)
            wi_hist = [0, 0]
            wi_te_hist = [0, 0]
            ve_xn2 = w.snap("VE")
            for m in range(2 * DI // 128):
                buf = m % 2
                pw = pA if buf == 0 else pB
                if buf == 0:
                    w.wait("sy", "TE", wi_te_hist[0] if m >= 2 else 0)
                    w.dma_sy(wint[buf][:, :],
                             win_bf[:, m * NKT_DM * 128:(m + 1) * NKT_DM * 128],
                             stream="I0")
                else:
                    w.wait("gp", "TE", wi_te_hist[0] if m >= 2 else 0)
                    w.dma_gp(wint[buf][:, :],
                             win_bf[:, m * NKT_DM * 128:(m + 1) * NKT_DM * 128],
                             stream="I1")
                w.wait("te", f"I{buf}")
                w.wait("te", "SE", wi_hist[0] if m >= 2 else w.snap("SE"))
                w.wait("te", "VE", ve_xn2)
                for kt in range(NKT_DM):
                    w.te_(lambda t, kt=kt, buf=buf, pw=pw: t.matmul(
                        pw[:, 0:TF], wint[buf][:, kt * 128:(kt + 1) * 128],
                        xnbf[:, kt * TF:(kt + 1) * TF],
                        start=(kt == 0), stop=(kt == NKT_DM - 1)))
                te_wi = w.snap("TE")
                wi_te_hist = [wi_te_hist[1], te_wi]
                w.wait("se", "TE", te_wi)
                w.wait("se", "ST")
                if m < DI // 128:
                    w.se_(lambda s, m=m, pw=pw: s.activation(
                        out=ubf_sb[:, :], in_=pw[:, 0:TF], func=AF.Silu,
                        bias=binu_c[:, m:m + 1]))
                    w.wait("se", "SE")
                    kag = (ci * TF) // (TPC // 4)
                    off = (ci * TF) % (TPC // 4)
                    w.dma_se(u_in2[kag][m * 128:(m + 1) * 128, off:off + TF],
                             ubf_sb[:, :])
                    wi_hist = [wi_hist[1], w.snap("SE")]
                else:
                    mz = m - DI // 128
                    w.se_(lambda s, pw=pw: s.copy(zbf_sb[:, :], pw[:, 0:TF]))
                    w.wait("se", "SE")
                    w.dma_se(z_d[mz * 128:(mz + 1) * 128, ci * TF:(ci + 1) * TF],
                             zbf_sb[:, :])
                    wi_hist = [wi_hist[1], w.snap("SE")]
            if ci % 2 == 1 and PIPE_AGU:
                w.wait("gp", "ST", w.snap("ST"))
                w.ag([u_in2[ci // 2].opt()], [u_ag2[ci // 2].opt()],
                     [[0, 1, 2, 3], [4, 5, 6, 7]])
        st_ab = w.snap("ST")
        if not PIPE_AGU:
            for kag in range(4):
                w.wait("gp", "ST", st_ab)
                w.ag([u_in2[kag].opt()], [u_ag2[kag].opt()],
                     [[0, 1, 2, 3], [4, 5, 6, 7]])

        for kt in range(NKT_DI):
            w.wait("sy", "SE")
            w.wait("sy", "VE")
            w.wait("sy", "TE")
            w.dma_sy(cast_f[:, 0:DG], W_dt_sl[kt * 128:(kt + 1) * 128, :])
            w.wait("se", "LD")
            w.se_(lambda s, kt=kt: s.copy(
                wdt_bf[:, kt * DG:(kt + 1) * DG], cast_f[:, 0:DG]))
        w.wait("se", "SE")
        se_wdt = w.snap("SE")

        hist_te_ufull = [0, 0]
        hist_se_ugcast = [0, 0]
        hist_ve_ugf = [0, 0]
        hist_se_sp = [se_wdt, se_wdt]
        hist_ve_sp = [[0, 0], [0, 0]]
        hist_se_bc = se_wdt
        red_hist = [0, 0]
        bc_hist = [0, 0]
        se_cast_hist = 0
        for scp in range(NCHS // 2):
          for cb2 in range(2):
            sc = 2 * scp + cb2
            t0 = sc * TS
            q_src = sc % 4
            jst = sc // 4
            kag = (jst * TS) // (TPC // 4)
            cofs = (jst * TS) % (TPC // 4)
            w.wait("sy", "AG", kag + 1)
            w.wait("sy", "TE", hist_te_ufull[cb2])
            w.wait("sy", "SE", hist_se_ugcast[0])
            w.wait("sy", "VE", hist_ve_ugf[cb2])
            src_ = u_ag2[kag][q_src * DI:(q_src + 1) * DI,
                              cofs:cofs + TS].rearrange(
                "(kt p) c -> p kt c", p=128)
            w.dma_sy(u_full[cb2][:, :].rearrange("p (kt c) -> p kt c", kt=NKT_DI),
                     src_, stream=f"U{cb2}")
          out_ap = bass.AP(af_t, ug_f[0].offset,
                           [[F32_COLS, 128], [ug_f[1].offset - ug_f[0].offset, 2],
                            [1, NDT * TS]])
          if scp == 0:
              w.wait("ve", "U0")
              w.wait("ve", "U1")
              w.wait("ve", "SE")
              if w.E == "ve":
                  nc.vector.tensor_copy(
                      out_ap,
                      bass.AP(ab_t, regs["vU0"],
                              [[BF_COLS, 128],
                               [u_full[1].offset - u_full[0].offset, 2],
                               [1, NDT * TS]]))
              w.ve_(lambda v: v.nop())
              se_ug = 0
              hist_se_ugcast[0] = 0
          else:
              w.wait("se", "U0")
              w.wait("se", "U1")
              w.wait("se", "VE", max(hist_ve_ugf[0], hist_ve_ugf[1]))
              if w.E == "se":
                  nc.scalar.copy(
                      out_ap,
                      bass.AP(ab_t, regs["sU0"],
                              [[BF_COLS, 128],
                               [u_full[1].offset - u_full[0].offset, 2],
                               [1, NDT * TS]]))
              w.se_(lambda s: s.nop())
              se_ug = w.snap("SE")
              hist_se_ugcast[0] = se_ug
          for cb in range(2):
            sc = 2 * scp + cb
            t0 = sc * TS
            for mt in range(NDT):
                pw = pA if mt % 2 == 0 else pB
                w.wait("te", f"U{cb}")
                w.wait("te", "SE", hist_se_sp[mt % 2])
                for kt in range(NKT_DI):
                    w.te_(lambda t, kt=kt, mt=mt, pw=pw, cb=cb: t.matmul(
                        pw[:, 0:TS],
                        wdt_bf[:, (kt * NDT + mt) * 128:(kt * NDT + mt) * 128 + 128],
                        u_full[cb][:, kt * TS:(kt + 1) * TS],
                        start=(kt == 0), stop=(kt == NKT_DI - 1)))
                te_dt = w.snap("TE")
                spb = spA[cb][mt % 2]
                w.wait("se", "TE", te_dt)
                w.wait("se", "VE", hist_ve_ugf[cb])
                w.wait("se", "VE", hist_ve_sp[cb][mt % 2])
                w.se_(lambda s, mt=mt, pw=pw, spb=spb: s.activation(
                    out=spb[:, 0:TS], in_=pw[:, 0:TS],
                    func=AF.Abs, bias=bdt_col[:, mt:mt + 1]))
                w.wait("se", "SE")
                w.se_(lambda s, spb=spb: s.activation(
                    out=spb[:, 0:TS], in_=spb[:, 0:TS],
                    func=AF.Exp, scale=-1.0))
                w.wait("se", "SE")
                w.se_(lambda s, spb=spb: s.activation(
                    out=spb[:, 0:TS], in_=spb[:, 0:TS],
                    func=AF.Ln, bias=cst[:, 1:2]))
                w.wait("se", "SE")
                w.se_(lambda s, mt=mt, pw=pw, cb=cb: s.activation(
                    out=dt_f[cb][:, mt * TS:(mt + 1) * TS], in_=pw[:, 0:TS],
                    func=AF.Relu, bias=bdt_col[:, mt:mt + 1]))
                se_sp = w.snap("SE")
                hist_se_sp[mt % 2] = se_sp
                w.wait("ve", "SE", se_sp)
                w.ve_(lambda v, mt=mt, cb=cb, spb=spb: v.tensor_add(
                    dt_f[cb][:, mt * TS:(mt + 1) * TS],
                    dt_f[cb][:, mt * TS:(mt + 1) * TS], spb[:, 0:TS]))
                hist_ve_sp[cb][mt % 2] = w.snap("VE")
            ve_dt = w.snap("VE")

            w.wait("te", "SE", hist_se_bc)
            for kt in range(NKT_DI):
                w.te_(lambda t, kt=kt, cb=cb: t.matmul(
                    pC[0:2 * DST, 0:TS], wx_bf[:, kt * 2 * DST:(kt + 1) * 2 * DST],
                    u_full[cb][:, kt * TS:(kt + 1) * TS],
                    start=(kt == 0), stop=(kt == NKT_DI - 1)))
            te_x = w.snap("TE")
            hist_te_ufull[cb] = te_x
            w.wait("se", "TE", te_x)
            w.wait("se", "ST")
            w.se_(lambda s, cb=cb: s.activation(out=bc_sb[cb][0:2 * DST, 0:TS],
                                                in_=pC[0:2 * DST, 0:TS],
                                                func=AF.Identity, bias=bxc[:, :]))
            hist_se_bc = w.snap("SE")
            w.wait("se", "SE")
            w.dma_se(bcT_d[:, t0:t0 + TS], bc_sb[cb][0:2 * DST, 0:TS])
            st_bc = w.snap("ST")

            w.wait("ve", "SE", se_ug)
            w.wait("ve", "VE", ve_dt)
            for dtl in range(NDT):
                w.ve_(lambda v, dtl=dtl, cb=cb: v.tensor_mul(
                    dtu_b[cb][:, dtl * TS:(dtl + 1) * TS],
                    dt_f[cb][:, dtl * TS:(dtl + 1) * TS],
                    ug_f[cb][:, dtl * TS:(dtl + 1) * TS]))
            ve_dtu = w.snap("VE")

            for sb in range(NSB):
                bb = sb % 2
                w.wait("sy", "ST", st_bc)
                w.wait("sy", "VE", bc_hist[bb])
                srcB = bcT_d[sb * SBK:(sb + 1) * SBK, t0:t0 + TS]
                srcB = srcB.rearrange("s (o c) -> s o c", o=1).broadcast_to(
                    [SBK, 128, TS]).transpose([1, 0, 2])
                w.dma_sy(brep[bb][:, :].rearrange("p (s c) -> p s c", s=SBK), srcB,
                         stream=f"B{bb}")
                srcC = bcT_d[DST + sb * SBK: DST + (sb + 1) * SBK, t0:t0 + TS]
                srcC = srcC.rearrange("s (o c) -> s o c", o=1).broadcast_to(
                    [SBK, 128, TS]).transpose([1, 0, 2])
                w.dma_sy(crep[bb][:, :].rearrange("p (s c) -> p s c", s=SBK), srcC,
                         stream=f"B{bb}")

                for dtl in range(NDT):
                    blk = sb * NDT + dtl
                    pb = blk % 2
                    dA_sl = dA_bufs[pb]
                    w.wait("se", "VE", red_hist[pb])
                    for j in range(SBK):
                        s_idx = sb * SBK + j
                        w.se_(lambda s, j=j, dtl=dtl, s_idx=s_idx, dA_sl=dA_sl,
                              cb=cb: s.activation(
                            out=dA_sl[:, j * TS:(j + 1) * TS],
                            in_=dt_f[cb][:, dtl * TS:(dtl + 1) * TS], func=AF.Exp,
                            scale=A_cols[:, dtl * DST + s_idx: dtl * DST + s_idx + 1]))
                    se_dA = w.snap("SE")
                    w.wait("ve", f"B{bb}")
                    w.wait("ve", "VE", ve_dtu)
                    for j in range(SBK):
                        w.ve_(lambda v, j=j, dtl=dtl, bb=bb, cb=cb: v.tensor_mul(
                            w_sl[:, j * TS:(j + 1) * TS],
                            dtu_b[cb][:, dtl * TS:(dtl + 1) * TS],
                            brep[bb][:, j * TS:(j + 1) * TS]))
                    ve_w = w.snap("VE")
                    w.wait("ve", "VE", ve_w)
                    w.wait("ve", "SE", se_dA)
                    for j in range(SBK):
                        col = dtl * DST + sb * SBK + j
                        w.ve_(lambda v, j=j, col=col, dA_sl=dA_sl:
                              v.tensor_tensor_scan(
                            h_sl[:, j * TS:(j + 1) * TS],
                            dA_sl[:, j * TS:(j + 1) * TS],
                            w_sl[:, j * TS:(j + 1) * TS],
                            hcar[:, col:col + 1], ALU.mult, ALU.add))
                    ve_scan = w.snap("VE")
                    w.wait("ve", "VE", ve_scan)
                    for j in range(SBK):
                        w.ve_(lambda v, j=j, dA_sl=dA_sl, bb=bb: v.tensor_mul(
                            dA_sl[:, j * TS:(j + 1) * TS],
                            h_sl[:, j * TS:(j + 1) * TS],
                            crep[bb][:, j * TS:(j + 1) * TS]))
                    ve_hC = w.snap("VE")
                    w.wait("ve", "VE", ve_hC)
                    w.wait("ve", "SE", se_cast_hist)
                    hhview = dA_sl[:, :].rearrange("p (s c) -> p c s", s=SBK)
                    if sb == 0:
                        w.ve_(lambda v, dtl=dtl: v.tensor_reduce(
                            yacc[:, dtl * TS:(dtl + 1) * TS], hhview, AX.X, ALU.add))
                    else:
                        w.ve_(lambda v: v.tensor_reduce(
                            yred[:, 0:TS], hhview, AX.X, ALU.add))
                        w.wait("ve", "VE")
                        w.ve_(lambda v, dtl=dtl: v.tensor_add(
                            yacc[:, dtl * TS:(dtl + 1) * TS],
                            yacc[:, dtl * TS:(dtl + 1) * TS], yred[:, 0:TS]))
                    ve_red = w.snap("VE")
                    red_hist[pb] = ve_red
                    w.wait("se", "VE", ve_red)
                    hl = h_sl[:, :].rearrange("p (s c) -> p s c", s=SBK)[:, :, TS - 1:TS]
                    hl = hl.rearrange("p s o -> p (s o)")
                    col0 = dtl * DST + sb * SBK
                    w.se_(lambda s, col0=col0, hl=hl: s.copy(
                        hcar[:, col0:col0 + SBK], hl))
                    if dtl == NDT - 1:
                        bc_hist[bb] = w.snap("VE")

            for dtl in range(NDT):
                w.wait("ve", "VE")
                w.ve_(lambda v, dtl=dtl, cb=cb: v.scalar_tensor_tensor(
                    yacc[:, dtl * TS:(dtl + 1) * TS],
                    ug_f[cb][:, dtl * TS:(dtl + 1) * TS], D_col[:, dtl:dtl + 1],
                    yacc[:, dtl * TS:(dtl + 1) * TS], ALU.mult, ALU.add))
            ve_y = w.snap("VE")
            hist_ve_ugf[cb] = ve_y
            w.wait("se", "VE", ve_y)
            w.wait("se", "ST")
            w.se_(lambda s: s.copy(ybf_s[:, :], yacc[:, 0:NDT * TS]))
            se_cast_hist = w.snap("SE")
            w.wait("se", "SE")
            mag = (sc * TS) // (T // 4)
            yofs = (sc * TS) % (T // 4)
            for dtl in range(NDT):
                w.dma_se(y_in2[mag][dtl * 128:(dtl + 1) * 128, yofs:yofs + TS],
                         ybf_s[:, dtl * TS:(dtl + 1) * TS])
          for mag in ([scp // 4] if (scp % 4 == 3 and PIPE_AGY) else
                      (range(4) if (scp == NCHS // 2 - 1 and not PIPE_AGY)
                       else [])):
            TQ = T // 4
            w.wait("gp", "ST", w.snap("ST"))
            w.ag([y_in2[mag].opt()], [y_ag2[mag].opt()],
                 [[0, 1, 2, 3], [4, 5, 6, 7]])
            w.wait("gp", "AG")
            nown = TQ // (4 * TF)
            for i in range(nown):
                w.wait("gp", "GD")
                w.dma_gp_dyn(lambda g, mag=mag, i=i: (
                    g.reg_alu(regs["t"], regs["y"],
                              mag * DI * TQ + i * 4 * TF, ALU.add),
                    g.dma_start(
                        out=gbulk[:, 0:NKT_DI * TF].rearrange(
                            "p (rt c) -> p rt c", rt=NKT_DI),
                        in_=bass.AP(y_ag2.tensor, regs["t"],
                                    [[TQ, 128], [128 * TQ, NKT_DI],
                                     [1, TF]])))[1])
                w.wait("gp", "GD")
                ci_loc = nown * mag + i
                w.dma_gp_dyn(lambda g, ci_loc=ci_loc: g.dma_start(
                    out=y_gs_d[:, ci_loc * TF:(ci_loc + 1) * TF].rearrange(
                        "(rt p) c -> p rt c", p=128),
                    in_=gbulk[:, 0:NKT_DI * TF].rearrange(
                        "p (rt c) -> p rt c", rt=NKT_DI)))
        gd_ybulk = w.snap("GD")
        w.gp_(lambda g: g.nop())
        gp_agy = w.snap("GP")

        w.wait("gp", "GD", gd_ybulk)
        for ci in range(NCHF):
            w.wait("sy", "VE")
            w.wait("sy", "SE")
            w.wait("sy", "TE")
            w.dma_sy(x1c[:, :].rearrange("p (kt c) -> p kt c", kt=NKT_DM),
                     x1_d[:, ci * TF:(ci + 1) * TF].rearrange(
                         "(kt p) c -> p kt c", p=128), stream="LDX")
            wo_te_hist = [0, 0]
            for kt in range(NKT_DI):
                kb = kt % 2
                w.wait("sy", "GD", gd_ybulk)
                w.wait("sy", "VE")
                w.wait("sy", "TE", wo_te_hist[1] if kt >= 1 else 0)
                w.dma_sy(ymk[:, :],
                         y_gs_d[kt * 128:(kt + 1) * 128, ci * TF:(ci + 1) * TF],
                         stream="LDX")
                w.wait("sy", "SE")
                w.dma_sy(zch[:, :],
                         z_d[kt * 128:(kt + 1) * 128, ci * TF:(ci + 1) * TF],
                         stream="LDX")
                w.wait("gp", "TE", wo_te_hist[1] if kt >= 1 else 0)
                w.dma_gp(wotk[kb][:, :],
                         wout_bf[kt * 128:(kt + 1) * 128, :], stream=f"O{kb}")
                w.wait("se", "LDX")
                w.wait("se", "VE")
                w.se_(lambda s, kt=kt: s.activation(
                    out=szt[:, :], in_=zch[:, :], func=AF.Silu,
                    bias=binz_c[:, kt:kt + 1]))
                se_sz = w.snap("SE")
                w.wait("ve", "SE", se_sz)
                w.wait("ve", "LDX")
                w.wait("ve", "TE")
                w.ve_(lambda v: v.tensor_mul(ymk[:, :], ymk[:, :], szt[:, :]))
                ve_ymk = w.snap("VE")
                w.wait("te", "VE", ve_ymk)
                w.wait("te", f"O{kb}")
                if kt == 0:
                    w.wait("te", "SE")
                for m in range(DM // 128):
                    w.te_(lambda t, m=m, kt=kt, kb=kb: t.matmul(
                        pALL[m][:, 0:TF], wotk[kb][:, m * 128:(m + 1) * 128],
                        ymk[:, :],
                        start=(kt == 0), stop=(kt == NKT_DI - 1)))
                wo_te_hist = [wo_te_hist[1], w.snap("TE")]
            te_wo = w.snap("TE")
            w.wait("ve", "TE", te_wo)
            w.wait("ve", "LDX")
            for m in range(DM // 128):
                w.wait("ve", "VE")
                w.ve_(lambda v, m=m: v.scalar_tensor_tensor(
                    x1c[:, m * TF:(m + 1) * TF], pALL[m][:, 0:TF],
                    bout_c[:, m:m + 1], x1c[:, m * TF:(m + 1) * TF],
                    ALU.add, ALU.add))
            ve_x2 = w.snap("VE")

            w.wait("se", "VE", ve_x2)
            w.se_(lambda s: s.copy(xbf[:, :], x1c[:, :]))
            ln_norm(x1c, xbf)
            ffn(x1c, x_res, ci)
            ve_out = w.snap("VE")

            for tt in range(NTT):
                pt = pE if tt % 2 == 0 else pF
                for half in range(2):
                    w.wait("te", "VE", ve_out)
                    w.wait("te", "SE")
                    for dk4 in range(4):
                        dk = half * 4 + dk4
                        w.te_(lambda t, tt=tt, dk=dk, dk4=dk4, pt=pt: t.transpose(
                            pt[:, dk4 * 128:(dk4 + 1) * 128],
                            x_res[:, dk * TF + tt * 128: dk * TF + tt * 128 + 128],
                            identt[:, :]))
                    te_t = w.snap("TE")
                    w.wait("se", "TE", te_t)
                    w.se_(lambda s, tt=tt, half=half, pt=pt: s.copy(
                        xtm[:, tt * DM + half * 512: tt * DM + half * 512 + 512],
                        pt[:, 0:512]))
            se_otm = w.snap("SE")
            w.wait("se", "SE", se_otm)
            for tt in range(NTT):
                w.dma_se(out_my[ci * TF + tt * 128: ci * TF + (tt + 1) * 128, :],
                         xtm[:, tt * DM:(tt + 1) * DM])
        st_out = w.snap("ST")

        w.wait("gp", "ST", st_out)
        w.wait("sy", "ST", st_out)

    with nc.Block() as block:
        walks = {E: Walk(nc, E, sems) for E in Walk.ENGS}

        def mk(E):
            def f(eng):
                walk(walks[E])
            return f
        block.sync(mk("sy"))
        block.gpsimd(mk("gp"))
        block.vector(mk("ve"))
        block.scalar(mk("se"))
        block.tensor(mk("te"))
    ref = walks["sy"].c
    for E in Walk.ENGS:
        assert walks[E].c == ref, f"walk divergence {E}: {walks[E].c} vs {ref}"
    return nc


def make_in_maps(cfg, inputs):
    x = np.ascontiguousarray(inputs["L0_attn_x"])
    TPC = cfg.TPC
    gate_wT = np.ascontiguousarray(np.transpose(
        inputs["L0_ffn_gate_w"].reshape(8, 4, HE, NKT_DM, 128),
        (4, 0, 1, 3, 2)).reshape(128, NE * NKT_DM * 128))
    up_wT = np.ascontiguousarray(np.transpose(
        inputs["L0_ffn_up_w"].reshape(8, 4, HE, NKT_DM, 128),
        (4, 0, 1, 3, 2)).reshape(128, NE * NKT_DM * 128))
    down_eT = np.ascontiguousarray(np.transpose(
        inputs["L0_ffn_down_w"].reshape(NE, DM // 128, 128, HE),
        (3, 1, 0, 2)).reshape(128, NE * DM))
    win_n = np.ascontiguousarray(np.transpose(
        inputs["L0_attn_W_in"].reshape(NKT_DM, 128, 2 * DI // 128, 128),
        (1, 2, 0, 3)).reshape(128, DM * 2 * DI // 128))
    sg_n = np.ascontiguousarray(np.transpose(
        inputs["L0_ffn_sgate_w"].reshape(HSH // 128, 128, NKT_DM, 128),
        (3, 0, 2, 1)).reshape(128, NKT_DM * HSH))
    su_n = np.ascontiguousarray(np.transpose(
        inputs["L0_ffn_sup_w"].reshape(HSH // 128, 128, NKT_DM, 128),
        (3, 0, 2, 1)).reshape(128, NKT_DM * HSH))
    sd_n = np.ascontiguousarray(np.transpose(
        inputs["L0_ffn_sdown_w"].reshape(DM // 128, 128, HSH // 128, 128),
        (3, 0, 2, 1)).reshape(128, (HSH // 128) * DM))
    r_n = np.ascontiguousarray(np.transpose(
        inputs["L0_ffn_router_w"].reshape(NE, NKT_DM, 128),
        (2, 1, 0)).reshape(128, NKT_DM * NE))
    b_in = inputs["L0_attn_b_in"]
    common = {
        "W_in": win_n,
        "b_in_u": np.ascontiguousarray(b_in[:DI].reshape(DI // 128, 128).T),
        "b_in_z": np.ascontiguousarray(b_in[DI:].reshape(DI // 128, 128).T),
        "W_x": np.ascontiguousarray(inputs["L0_attn_W_x"]),
        "b_x": np.ascontiguousarray(inputs["L0_attn_b_x"].reshape(2 * DST, 1)),
        "W_out": np.ascontiguousarray(inputs["L0_attn_W_out"]),
        "b_out_c": np.ascontiguousarray(
            inputs["L0_attn_b_out"].reshape(DM // 128, 128).T),
        "r_wT": r_n,
        "r_b": np.ascontiguousarray(inputs["L0_ffn_router_bias"].reshape(NE, 1)),
        "gate_wT": gate_wT,
        "up_wT": up_wT,
        "down_eT": down_eT,
        "sg_wT": sg_n,
        "su_wT": su_n,
        "sd_wT": sd_n,
        "ident": np.eye(128, dtype=np.float32),
        "ones_row": np.ones((1, 128), dtype=np.float32),
    }
    W_dt = inputs["L0_attn_W_dt"]
    b_dt = inputs["L0_attn_b_dt"]
    A_log = inputs["L0_attn_A_log"]
    D_par = inputs["L0_attn_D_param"]
    in_maps = []
    for c in range(NCORE):
        b, q = c // 4, c % 4
        g = q
        m = dict(common)
        m["x_my"] = np.ascontiguousarray(
            x[b].reshape(-1, 4, cfg.TF, DM)[:, q].reshape(TPC, DM))
        m["W_dt_sl"] = np.ascontiguousarray(W_dt[:, g * cfg.DG:(g + 1) * cfg.DG])
        m["b_dt_sl"] = np.ascontiguousarray(
            b_dt[g * cfg.DG:(g + 1) * cfg.DG].reshape(cfg.NDT, 128).T)
        m["A_log_sl"] = np.ascontiguousarray(A_log[g * cfg.DG:(g + 1) * cfg.DG, :])
        m["D_sl"] = np.ascontiguousarray(
            D_par[g * cfg.DG:(g + 1) * cfg.DG].reshape(cfg.NDT, 128).T)
        m["cinfo"] = np.array(
            [[g * cfg.NDT * cfg.TS, q * cfg.TF, 0, 0]], dtype=np.uint32)
        in_maps.append(m)
    return in_maps


def kernel(**inputs):
    cfg = CFG()
    x = inputs["L0_attn_x"]
    nc = build_program(cfg)
    in_maps = make_in_maps(cfg, inputs)
    res = run_bass_kernel_spmd(nc, in_maps, core_ids=list(range(NCORE)))
    out = np.empty_like(np.ascontiguousarray(x))
    TPC = cfg.TPC
    for c in range(NCORE):
        b, q = c // 4, c % 4
        out[b].reshape(-1, 4, cfg.TF, DM)[:, q] = \
            res.results[c]["out_my"].reshape(-1, cfg.TF, DM)
    return out
